# version 20
# baseline (speedup 1.0000x reference)
import numpy as np
import ml_dtypes
import concourse.bass as bass
import concourse.mybir as mybir
import concourse.bass_utils as _bass_utils
from concourse.bass_utils import run_bass_kernel_spmd
from contextlib import ExitStack

if not getattr(_bass_utils, "_ldwopt_patched", False):
    _orig_run_command = _bass_utils.run_command

    def _run_command_ldwopt(cmd, *a, **kw):
        try:
            cmd = ["--enable-ldw-opt=true" if c == "--enable-ldw-opt=false" else c for c in cmd]
        except TypeError:
            pass
        return _orig_run_command(cmd, *a, **kw)

    _bass_utils.run_command = _run_command_ldwopt
    _bass_utils._ldwopt_patched = True

F32 = mybir.dt.float32
BF16 = mybir.dt.bfloat16
I32 = mybir.dt.int32
I8 = mybir.dt.int8
AF = mybir.ActivationFunctionType
OP = mybir.AluOpType
AX = mybir.AxisListType.X

P = 128
D = 1024
SEQ = 2048
NB = 2
NT = 16
NTT = NB * NT
NCH = 8
EPS = 1e-5
N_CORES = 8


def build():
    nc = bass.Bass("TRN2", target_bir_lowering=False)

    x_in = nc.dram_tensor("x_in", [NB, SEQ, D], F32, kind="ExternalInput")
    wq_in = nc.dram_tensor("wq_in", [D, D], F32, kind="ExternalInput")
    w_in = nc.dram_tensor("w_in", [D, D], F32, kind="ExternalInput")
    gamma_in = nc.dram_tensor("gamma_in", [1, D], F32, kind="ExternalInput")
    beta_in = nc.dram_tensor("beta_in", [1, D], F32, kind="ExternalInput")
    out_d = nc.dram_tensor("out_d", [NB, SEQ, D], F32, kind="ExternalOutput")
    xn2_d = nc.dram_tensor("xn2_d", [NB, SEQ, D], F32)
    cc_in0 = nc.dram_tensor("cc_in0", [1, 8], F32)
    cc_out0 = nc.dram_tensor("cc_out0", [1, 8], F32)
    cc_in1 = nc.dram_tensor("cc_in1", [1, 8], F32)
    cc_out1 = nc.dram_tensor("cc_out1", [1, 8], F32)
    delta_d = nc.dram_tensor("delta_d", [1, 1], F32)
    al_d = nc.dram_tensor("al_d", [1, 2], F32)
    csa_d = nc.dram_tensor("csa_d", [1, 2], F32)
    identb_d = nc.inline_tensor(np.eye(P, dtype=ml_dtypes.bfloat16), name="identb")
    identf_d = nc.inline_tensor(np.eye(P, dtype=np.float32), name="identf")
    eps_dd = nc.inline_tensor(
        np.concatenate([np.full((P, 1), EPS, np.float32),
                        np.full((P, 1), 1.0 / D, np.float32),
                        np.full((P, 1), EPS * EPS, np.float32)], axis=1), name="epsc")

    SCHED = {"sync": [], "v": [], "s": [], "t": [], "g": []}
    DMA_FAM = {"qs": 4, "qo": 2, "qa": 4, "qg": 2}
    SEM_NAMES = ["v", "s", "t", "g", "cc"] + [f + str(i) for f, n in DMA_FAM.items() for i in range(n)]
    CTR = {n: 0 for n in SEM_NAMES}
    FAM_ROT = {f: 0 for f in DMA_FAM}

    def op(eng, fn, waits=(), inc=None, inc_n=1):
        waits = [w for w in waits if w is not None]
        assert all(callable(w) or (isinstance(w, tuple) and len(w) == 2) for w in waits)
        if inc in DMA_FAM:
            fam = inc
            inc = fam + str(FAM_ROT[fam] % DMA_FAM[fam])
            FAM_ROT[fam] += 1
            waits.append((inc, CTR[inc]))
        tok = None
        if inc is not None:
            CTR[inc] += inc_n
            tok = (inc, CTR[inc])
        SCHED[eng].append((tuple(waits), fn, inc, CTR[inc] if inc else 0, inc_n))
        return tok

    def now(sem):
        return (sem, CTR[sem])

    es = ExitStack()
    with es:
        S_ = lambda nm, shp, dt=F32: es.enter_context(nc.sbuf_tensor(nm, shp, dt))
        xnT = S_("xnT", [P, NTT * D], BF16)
        gA = S_("gA", [P, NCH * D], BF16)
        gB = S_("gB", [P, NCH * D], BF16)
        gC = S_("gC", [P, NCH * D], BF16)
        wq_bf = S_("wq_bf", [P, NCH * D], BF16)
        tt_bf = S_("tt_bf", [P, NCH * D], BF16)
        gb_g = S_("gb_g", [P, D])
        gb_b = S_("gb_b", [P, D])
        xg = [S_(f"xg{i}", [P, D]) for i in range(3)]
        xn_t = [S_(f"xn_t{i}", [P, D]) for i in range(3)]
        xn2_t = [S_("xn2a", [P, D]), S_("xn2b", [P, D])]
        g_t = [S_("g0", [P, D]), S_("g1", [P, D])]
        rint_t = S_("rint", [P, D], I8)
        r_bf = S_("rbf", [P, D], BF16)
        xnf_t = [S_("xnf0", [P, D], BF16), S_("xnf1", [P, D], BF16)]
        rt_t = [S_("rt0", [P, D], BF16), S_("rt1", [P, D], BF16)]
        s_sb = S_("ssb", [P, D], BF16)
        bn6 = [S_("bn6", [P, 12])] * 2
        stt = S_("stt", [P, 48])
        amall = S_("amall", [P, 32])
        awsums = S_("awsums", [P, 8])
        nums = S_("nums", [P, 8])
        dens = S_("dens", [P, 8])
        small = S_("small", [P, 14])
        csa = S_("csa", [P, 2])
        delta_t = S_("delta_t", [P, 4])
        identb = S_("identb_s", [P, P], BF16)
        identf = S_("identf_s", [P, P])

        pA = es.enter_context(nc.psum_tensor("pA", [P, D], F32))
        pB = es.enter_context(nc.psum_tensor("pB", [P, D], F32))
        pC = es.enter_context(nc.psum_tensor("pC", [P, D], F32))
        pT0 = [es.enter_context(nc.psum_tensor("pT0a", [P, D], BF16)),
               es.enter_context(nc.psum_tensor("pT0b", [P, D], BF16))]
        pG = [pA, pB, pC]

        SEM = {}
        for s in SEM_NAMES:
            SEM[s] = es.enter_context(nc.semaphore("sem_" + s))
        block = es.enter_context(nc.Block())

        op("g", lambda e: e.dma_start(out=identb[:], in_=identb_d[:]), inc="qg", inc_n=16)
        op("g", lambda e: e.dma_start(out=identf[:], in_=identf_d[:]), inc="qg", inc_n=16)
        op("g", lambda e: e.dma_start(out=delta_t[:, 1:4], in_=eps_dd[:]), inc="qg", inc_n=16)
        op("g", lambda e: e.dma_start(out=gb_g[:], in_=gamma_in.ap().partition_broadcast(P)), inc="qg", inc_n=16)
        op("g", lambda e: e.dma_start(out=gb_b[:], in_=beta_in.ap().partition_broadcast(P)), inc="qg", inc_n=16)
        tok_consts = (now("qg0"), now("qg1"))

        tok_absw = [None] * NCH
        tok_wld = [None] * NCH
        for c in range(NCH):
            par = c % 2
            tok_wld[c] = op("sync", lambda e, c=c, par=par: e.dma_start(
                out=g_t[par][:], in_=w_in.ap()[c * P:(c + 1) * P, :]),
                waits=[tok_absw[c - 2] if c >= 2 else None], inc="qs", inc_n=16)
            tok_absw[c] = op("s", lambda e, c=c, par=par: e.activation(
                out=xn_t[0][:], in_=g_t[par][:], func=AF.Abs, accum_out=awsums[:, c:c + 1]),
                waits=[tok_wld[c], (tok_absw[c - 1] if c >= 1 else None)], inc="s")
        tok_rsum = op("v", lambda e: e.reduce_sum(out=small[:, 0:1], in_=awsums[:, 0:8], axis=AX),
                      waits=[tok_absw[NCH - 1]], inc="v")
        tok_trs = op("t", lambda e: e.transpose(pA[0:1, 0:P], small[:, 0:1], identf[:]),
                     waits=[tok_rsum, tok_consts[0], tok_consts[1]], inc="t")
        tok_tot = op("v", lambda e: e.reduce_sum(out=small[0:1, 1:2], in_=pA[0:1, 0:P], axis=AX),
                     waits=[tok_trs], inc="v")
        tok_del = op("v", lambda e: e.tensor_scalar(out=small[0:1, 2:3], in0=small[0:1, 1:2],
                                                    scalar1=float(0.7 / (D * D)), scalar2=None, op0=OP.mult),
                     waits=[tok_tot], inc="v")
        tok_dst = op("g", lambda e: e.dma_start(out=delta_d[0:1, 0:1], in_=small[0:1, 2:3]),
                     waits=[tok_del], inc="qg", inc_n=16)
        tok_dbk = op("g", lambda e: e.dma_start(out=delta_t[:, 0:1], in_=delta_d.ap()[0:1, 0:1].partition_broadcast(P)),
                     waits=[tok_dst], inc="qg", inc_n=16)

        tok_T = [None] * NCH
        tok_trT = [None] * NCH
        tok_evT = [None] * NCH
        tok_num = [None] * NCH
        tok_sgn = [None] * NCH
        tok_mkl = [None] * NCH
        tok_wld2 = [None] * NCH
        absw_b = [xn_t[0], xn_t[2]]
        sgn_b = [r_bf, rt_t[1]]

        def p2_s0(c):
            par = c % 2
            tok_wld2[c] = op("sync", lambda e, c=c, par=par: e.dma_start(
                out=g_t[par][:], in_=w_in.ap()[c * P:(c + 1) * P, :]),
                waits=[tok_sgn[c - 2] if c >= 2 else tok_absw[NCH - 1],
                       tok_mkl[c - 2] if c >= 2 else None],
                inc="qs", inc_n=16)

        def p2_s1(c):
            par = c % 2
            op("s", lambda e, c=c, par=par: e.activation(
                out=absw_b[par][:], in_=g_t[par][:], func=AF.Abs),
                waits=[tok_wld2[c], tok_num[c - 2] if c >= 2 else tok_rsum], inc="s")
            tok_sgn[c] = op("s", lambda e, c=c, par=par: e.activation(
                out=sgn_b[par][:], in_=g_t[par][:], func=AF.Sign),
                waits=[tok_T[c - 2] if c >= 2 else None], inc="s")

        def p2_s2(c):
            par = c % 2
            tok_mkl[c] = op("v", lambda e, c=c, par=par: e.tensor_scalar(
                out=xnf_t[0][:], in0=absw_b[par][:], scalar1=delta_t[:, 0:1], scalar2=0.0,
                op0=OP.is_gt, op1=OP.add, accum_out=dens[:, c:c + 1]),
                waits=[tok_sgn[c], tok_dbk, tok_T[c - 1] if c >= 1 else None], inc="v")
            tok_num[c] = op("v", lambda e, c=c, par=par: e.scalar_tensor_tensor(
                out=xn_t[1][:], in0=absw_b[par][:], scalar=1.0, in1=xnf_t[0][:],
                op0=OP.mult, op1=OP.mult, accum_out=nums[:, c:c + 1]),
                waits=[tok_mkl[c]], inc="v")
            tok_T[c] = op("v", lambda e, c=c, par=par: e.tensor_tensor(
                out=rt_t[0][:], in0=sgn_b[par][:], in1=xnf_t[0][:], op=OP.mult),
                waits=[tok_num[c], tok_trT[c - 1] if c >= 1 else None], inc="v")

        def p2_s3(c):
            for j in range(NCH):
                tok_trT[c] = op("t", lambda e, j=j: e.transpose(
                    pT0[0][:, j * P:(j + 1) * P], rt_t[0][:, j * P:(j + 1) * P], identb[:]),
                    waits=[tok_T[c] if j == 0 else None,
                           tok_evT[c - 1] if (j == 0 and c >= 1) else None,
                           (tok_trs if (j == 0 and c == 0) else None)],
                    inc="t" if j == NCH - 1 else None)
            tok_evT[c] = op("s", lambda e, c=c: e.copy(
                out=tt_bf[:].rearrange("p (j o) -> p j o", o=D)[:, :, c * P:(c + 1) * P],
                in_=pT0[0][:].rearrange("p (j o) -> p j o", o=P)),
                waits=[tok_trT[c]], inc="s")

        for i in range(NCH + 3):
            if i < NCH:
                p2_s0(i)
            if 0 <= i - 1 < NCH:
                p2_s1(i - 1)
            if 0 <= i - 3 < NCH:
                p2_s3(i - 3)
            if 0 <= i - 2 < NCH:
                p2_s2(i - 2)

        tok_ns = op("v", lambda e: e.reduce_sum(out=small[:, 3:4], in_=nums[:, 0:8], axis=AX),
                    waits=[tok_num[NCH - 1]], inc="v")
        tok_ds = op("v", lambda e: e.reduce_sum(out=small[:, 4:5], in_=dens[:, 0:8], axis=AX),
                    waits=[tok_ns], inc="v")
        tok_tr2 = op("t", lambda e: e.transpose(pA[0:2, 0:P], small[:, 3:5], identf[:]),
                     waits=[tok_ds], inc="t")
        tok_nd = op("v", lambda e: e.reduce_sum(out=small[0:2, 5:6], in_=pA[0:2, 0:P], axis=AX),
                    waits=[tok_tr2], inc="v")
        tok_alst = op("g", lambda e: e.dma_start(out=al_d[0:1, 0:2], in_=small[0:2, 5:6]),
                      waits=[tok_nd], inc="qg", inc_n=16)
        tok_albk = op("g", lambda e: e.dma_start(out=small[0:1, 6:8], in_=al_d[0:1, 0:2]),
                      waits=[tok_alst], inc="qg", inc_n=16)
        tok_dmx = op("v", lambda e: e.tensor_scalar(out=small[0:1, 8:9], in0=small[0:1, 7:8],
                                                    scalar1=1.0, scalar2=None, op0=OP.max),
                     waits=[tok_albk], inc="v")
        tok_rd = op("v", lambda e: e.reciprocal(out=small[0:1, 9:10], in_=small[0:1, 8:9]),
                    waits=[tok_dmx], inc="v")
        tok_alpha = op("v", lambda e: e.tensor_tensor(out=small[0:1, 10:11], in0=small[0:1, 6:7],
                                                      in1=small[0:1, 9:10], op=OP.mult),
                       waits=[tok_rd], inc="v")

        tok_wqc = [None] * NCH
        tok_wqc[NCH - 1] = op("g", lambda e: e.dma_start(
            out=wq_bf[:].rearrange("p (c i) -> p c i", i=D),
            in_=wq_in.ap().rearrange("(c p) i -> p c i", p=P)),
            inc="qg", inc_n=16)
        tok_wql = tok_wqc[NCH - 1]

        setup_v, setup_s, setup_t = now("v"), now("s"), now("t")

        tok_x = {}
        tok_aggrD = {}
        tok_ti = {}
        tok_lv2 = {}
        tok_diff = {}
        tok_txn = {}
        tok_cast = {}
        tok_r2D = {}
        tok_u = {}
        tok_mul = {}
        tok_xn2 = {}
        tok_am = {}
        tok_gst = {}
        tok_gstop = {}
        tok_gev = {}
        tok_mstop = {}
        tok_mev = {}
        tok_ccgo = {}
        tok_ccred = {}
        tok_gm_hold = {}

        def stC(k, lo, hi):
            return stt[:, (k % 4) * 12 + lo:(k % 4) * 12 + hi]

        def a_load(b, k):
            par = (b * NT + k) % 3
            tok_x[(b, k)] = op("sync", lambda e, b=b, k=k, par=par: e.dma_start(
                out=xg[par][:], in_=x_in.ap()[b][k * P:(k + 1) * P, :]),
                waits=[tok_txn.get((b, k - 3)) if k >= 3 else (tok_txn.get((b - 1, NT - 3 + k)) if b else setup_v),
                       tok_cast.get((b, k - 3)) if k >= 3 else (tok_cast.get((b - 1, NT - 3 + k)) if b else setup_s),
                       tok_x.get((b, k - 3)) if k >= 3 else (tok_x.get((b - 1, NT - 3 + k)) if b else None)],
                inc="qs", inc_n=16)

        def a_s1(b, k):
            par = k % 2
            t1 = op("v", lambda e, par=par, b=b, k=k: e.bn_stats(out=bn6[par][:, 0:6], in_=xg[(b * NT + k) % 3][:, 0:512]),
                    waits=[tok_x[(b, k)],
                           tok_aggrD.get((b, k - 1)) or (tok_aggrD.get((b - 1, NT - 1)) if b else None)], inc="v")
            t2 = op("v", lambda e, par=par, b=b, k=k: e.bn_stats(out=bn6[par][:, 6:12], in_=xg[(b * NT + k) % 3][:, 512:1024]),
                    waits=[], inc="v")
            tok_aggrD[(b, k)] = op("v", lambda e, par=par, k=k: e.bn_aggr(out=stC(k, 0, 2), in_=bn6[par][:, 0:12]),
                                   waits=[t2, tok_r2D.get((b, k - 4)) or (tok_r2D.get((b - 1, NT - 4 + k)) if b else None)],
                                   inc="v")

        def a_s2(b, k):
            tl = op("s", lambda e, k=k: e.activation(out=stC(k, 2, 3), in_=stC(k, 1, 2),
                                                     func=AF.Ln, bias=delta_t[:, 1:2]),
                    waits=[tok_aggrD[(b, k)],
                           tok_diff.get((b, k - 4)) or (tok_diff.get((b - 1, NT - 4 + k)) if b else None)], inc="s")
            tok_ti[(b, k)] = op("s", lambda e, k=k: e.activation(out=stC(k, 3, 4), in_=stC(k, 2, 3),
                                                                 func=AF.Exp, scale=-0.5),
                                waits=[tl], inc="s")
            tok_lv2[(b, k)] = op("s", lambda e, k=k: e.activation(out=stC(k, 4, 5), in_=stC(k, 1, 2),
                                                                  func=AF.Ln, scale=float(1.0 + EPS),
                                                                  bias=delta_t[:, 3:4]),
                                waits=[tok_ti[(b, k)]], inc="s")

        def a_s3(b, k):
            tok_txn[(b, k)] = op("g", lambda e, b=b, k=k: e.tensor_scalar(
                out=xn_t[(b * NT + k) % 3][:], in0=xg[(b * NT + k) % 3][:], scalar1=stC(k, 0, 1), scalar2=stC(k, 3, 4),
                op0=OP.subtract, op1=OP.mult),
                waits=[tok_ti[(b, k)],
                       tok_u.get((b, k - 3)) or (tok_u.get((b - 1, NT - 3 + k)) if b else None),
                       setup_s if (b == 0 and k == 0) else None], inc="g")
            tok_diff[(b, k)] = op("v", lambda e, k=k: e.tensor_tensor(
                out=stC(k, 5, 6), in0=stC(k, 2, 3), in1=stC(k, 4, 5), op=OP.subtract),
                waits=[tok_lv2[(b, k)],
                       tok_r2D.get((b, k - 4)) or (tok_r2D.get((b - 1, NT - 4 + k)) if b else None)], inc="v")

        def a_s4(b, k):
            boff = b * NT * D
            tok_cast[(b, k)] = op("s", lambda e, b=b, k=k, boff=boff: e.activation(
                out=xnT[:, boff + k * D: boff + (k + 1) * D], in_=xn_t[(b * NT + k) % 3][:],
                func=AF.Copy),
                waits=[tok_txn[(b, k)]], inc="s")
            tok_r2D[(b, k)] = op("s", lambda e, k=k: e.activation(
                out=stC(k, 6, 7), in_=stC(k, 5, 6), func=AF.Exp, scale=0.5),
                waits=[tok_cast[(b, k)], tok_diff[(b, k)],
                       tok_u.get((b, k - 4)) or (tok_u.get((b - 1, NT - 4 + k)) if b else None)],
                inc="s")

        def a_s7(b, k):
            par = k % 2
            tok_u[(b, k)] = op("v", lambda e, b=b, k=k, par=par: e.scalar_tensor_tensor(
                out=xn2_t[par][:], in0=xn_t[(b * NT + k) % 3][:], scalar=stC(k, 6, 7),
                in1=gb_g[:], op0=OP.mult, op1=OP.mult),
                waits=[tok_r2D[(b, k)],
                       tok_gst.get((b, k - 2)) if k >= 2 else (tok_gst.get((b - 1, NT - 2 + k)) if b else None),
                       tok_consts[0] if (b == 0 and k == 0) else None,
                       tok_consts[1] if (b == 0 and k == 0) else None], inc="v")
            tok_xn2[(b, k)] = op("g", lambda e, par=par: e.tensor_tensor(
                out=xn2_t[par][:], in0=xn2_t[par][:], in1=gb_b[:], op=OP.add),
                waits=[tok_u[(b, k)]], inc="g")
        def a_s8(b, k):
            par = k % 2
            tok_am[(b, k)] = op("v", lambda e, par=par, b=b, k=k: e.tensor_reduce(
                out=amall[:, b * NT + k: b * NT + k + 1], in_=xn2_t[par][:], axis=AX,
                op=OP.max, apply_absolute_value=True),
                waits=[tok_xn2[(b, k)]], inc="v")
            tok_gst[(b, k)] = op("s", lambda e, b=b, k=k, par=par: e.dma_start(
                out=xn2_d.ap()[b][k * P:(k + 1) * P, :], in_=xn2_t[par][:]),
                waits=[tok_am[(b, k)]], inc="qa", inc_n=16)

        def a_pe(b, k):
            boff = b * NT * D
            for c in range(3):
                for h in range(2):
                    last = (k == NT - 1 and h == 1)
                    tok = op("t", lambda e, b=b, k=k, c=c, h=h, boff=boff: e.matmul(
                        pG[c][:, h * 512:(h + 1) * 512],
                        xnT[:, boff + k * D + c * P: boff + k * D + (c + 1) * P],
                        xnT[:, boff + k * D + h * 512: boff + k * D + (h + 1) * 512],
                        start=(k == 0), stop=(k == NT - 1)),
                        waits=[tok_cast[(b, k)] if (c == 0 and h == 0) else None,
                               (tok_gev.get((b - 1, 7)) if (b and k == 0 and c == 0 and h == 0) else None),
                               (tok_mev.get((b - 1, NCH - 1)) if (b and k == 0 and c == 0 and h == 0) else None),
                               (tok_ccred.get(b - 1) if (b and k == 0 and c == 0 and h == 0) else None),
                               (setup_t if (b == 0 and k == 0 and c == 0 and h == 0) else None)],
                        inc="t" if last else None)
                    if last:
                        tok_gstop[(b, c)] = tok

        def phaseA(b, defer=None):
            defer = defer or []
            for i in range(NT + 5):
                if i < NT:
                    a_load(b, i)
                if 0 <= i - 1 < NT:
                    a_s1(b, i - 1)
                    a_s2(b, i - 1)
                if 0 <= i - 2 < NT:
                    a_s3(b, i - 2)
                    a_s4(b, i - 2)
                    if b == 0:
                        a_pe(b, i - 2)
                if 0 <= i - 4 < NT:
                    a_s7(b, i - 4)
                if 0 <= i - 5 < NT:
                    a_s8(b, i - 5)
                if i >= 1 and defer:
                    defer.pop(0)()
            while defer:
                defer.pop(0)()

        def g_rest_m_cc(b, defer=None):
            boff = b * NT * D
            gdst = gA
            mdst = gB if b == 0 else gC
            ccin = cc_in0 if b == 0 else cc_in1
            ccout = cc_out0 if b == 0 else cc_out1
            DQ = defer if defer is not None else []
            def emit(fn):
                if defer is None:
                    fn()
                else:
                    DQ.append(fn)
            if b == 0:
                for c in range(3):
                    emit(lambda c=c: tok_gev.__setitem__((b, c), op("s", lambda e, c=c, gdst=gdst: e.copy(
                        out=gdst[:, c * D:(c + 1) * D], in_=pG[c][:]),
                        waits=[tok_gstop[(b, c)]], inc="s")))
            hold = {}
            def _amr():
                hold['amr'] = op("v", lambda e, b=b: e.reduce_max(out=small[:, 11:12],
                                                                  in_=amall[:, b * NT:(b + 1) * NT], axis=AX),
                                 waits=[tok_am[(b, NT - 1)],
                                        (tok_gm_hold[0]['amt'] if b == 1 else None)], inc="v")
            emit(_amr)
            if b == 0:
                hold['amt'] = op("t", lambda e, b=b: e.transpose(pC[0:1, 0:P], small[:, 11:12], identf[:]),
                                 waits=[lambda: hold['amr'], lambda: tok_gev[(0, 2)]], inc="t")

            def _gm():
                hold['gm'] = op("v", lambda e, b=b: e.reduce_max(out=small[0:1, 12:13],
                                                                 in_=pC[0:1, 0:P], axis=AX),
                                waits=[lambda: hold['amt'],
                                       (tok_ccred[0] if b == 1 else None)], inc="v")
                tok_ccred[b] = op("v", lambda e, b=b: e.tensor_copy(
                    out=nums[0:1, 0:8],
                    in_=small[0:1, 12:13].broadcast_to((1, 8))),
                    waits=[hold['gm'], tok_ccgo.get(b - 1), (tok_ns if b == 0 else None)], inc="v")
                cci = op("g", lambda e, ccin=ccin: e.dma_start(out=ccin[0:1, 0:8], in_=nums[0:1, 0:8]),
                         waits=[tok_ccred[b]], inc="qg", inc_n=16)
                tok_ccgo[b] = op("g", lambda e, ccin=ccin, ccout=ccout: e.collective_compute(
                    "AllReduce", OP.max, replica_groups=[list(range(N_CORES))],
                    ins=[ccin.ap().opt()], outs=[ccout.ap().opt()]),
                    waits=[cci], inc="cc", inc_n=1)
            emit(_gm)
            tok_gm_hold[b] = hold
            rem = list(range(3, NCH)) if b == 0 else list(range(NCH))
            BANKS = [0, 1, 0, 1, 2, 0, 1, 2, 0, 1, 2, 0, 1, 2, 0, 1]
            first_pc = BANKS.index(2)
            for bi, c in enumerate(rem):
                if bi == first_pc and b == 1:
                    hold['amt'] = op("t", lambda e, b=b: e.transpose(pC[0:1, 0:P], small[:, 11:12], identf[:]),
                                     waits=[lambda: hold['amr'], lambda: tok_mev[(0, NCH - 1)]],
                                     inc="t")
                pm = pG[BANKS[bi]]
                previ = None
                for q in range(bi - 1, -1, -1):
                    if BANKS[q] == BANKS[bi]:
                        previ = q
                        break
                if previ is not None:
                    prev_user = (lambda b=b, c=rem[previ]: tok_gev[(b, c)])
                elif b == 0:
                    prev_user = (lambda bk=BANKS[bi]: tok_gev[(0, bk)])
                else:
                    prev_user = (lambda: tok_mev[(0, NCH - 1)])
                for k in range(NT):
                    for h in range(2):
                        last = (k == NT - 1 and h == 1)
                        tok = op("t", lambda e, k=k, c=c, h=h, pm=pm, boff=boff: e.matmul(
                            pm[:, h * 512:(h + 1) * 512],
                            xnT[:, boff + k * D + c * P: boff + k * D + (c + 1) * P],
                            xnT[:, boff + k * D + h * 512: boff + k * D + (h + 1) * 512],
                            start=(k == 0), stop=(k == NT - 1)),
                            waits=[prev_user if (k == 0 and h == 0) else None,
                                   (hold['amt'] if (k == 0 and h == 0 and bi == first_pc) else None),
                                   ((lambda b=b: tok_gm_hold[b]['gm']) if (k == 0 and h == 0 and bi == first_pc) else None),
                                   (tok_cast[(b, NT - 1)] if (k == 0 and h == 0 and bi == 0 and b == 1) else None)],
                            inc="t" if last else None)
                        if last:
                            tok_gstop[(b, c)] = tok
                emit(lambda c=c, pm=pm: tok_gev.__setitem__((b, c), op(
                    "s", lambda e, c=c, pm=pm, gdst=gdst: e.copy(
                        out=gdst[:, c * D:(c + 1) * D], in_=pm[:]),
                    waits=[tok_gstop[(b, c)]], inc="s")))
            for hc in range(NCH):
                pm = pG[BANKS[(len(rem) + hc) % len(BANKS)]]
                for oc in range(NCH):
                    for h in range(2):
                        last = (oc == NCH - 1 and h == 1)
                        tok = op("t", lambda e, hc=hc, oc=oc, h=h, pm=pm, gdst=gdst: e.matmul(
                            pm[:, h * 512:(h + 1) * 512],
                            wq_bf[:, oc * D + hc * P: oc * D + (hc + 1) * P],
                            gdst[:, oc * D + h * 512: oc * D + (h + 1) * 512],
                            start=(oc == 0), stop=(oc == NCH - 1)),
                            waits=[((lambda b=b: tok_gev[(b, NCH - 1)]) if (hc == 0 and oc == 0 and h == 0) else None),
                                   (tok_wqc[NCH - 1] if (hc == 0 and oc == 0 and h == 0) else None),
                                   ((lambda b=b, hc=hc: tok_gev[(b, NCH - 3 + hc)]) if (hc in (1, 2) and oc == 0 and h == 0) else None),
                                   ((lambda b=b, hc=hc: tok_mev[(b, hc - 3)]) if (hc >= 3 and oc == 0 and h == 0) else None)],
                            inc="t" if last else None)
                        if last:
                            tok_mstop[(b, hc)] = tok
                emit(lambda hc=hc, pm=pm: tok_mev.__setitem__((b, hc), op(
                    "s", lambda e, hc=hc, pm=pm, mdst=mdst: e.copy(
                        out=mdst[:, hc * D:(hc + 1) * D], in_=pm[:]),
                    waits=[tok_mstop[(b, hc)]], inc="s")))

        phaseA(0)
        DQ0 = []
        g_rest_m_cc(0, defer=DQ0)
        phaseA(1, defer=DQ0)
        g_rest_m_cc(1)

        gl0 = op("g", lambda e: e.dma_start(out=small[0:1, 13:14], in_=cc_out0[0:1, 0:1]),
                 waits=[tok_ccgo[0]], inc="qg", inc_n=16)
        gl1 = op("g", lambda e: e.dma_start(out=awsums[0:1, 0:1], in_=cc_out1[0:1, 0:1]),
                 waits=[tok_ccgo[1], tok_ccred[1]], inc="qg", inc_n=16)
        gmx = op("v", lambda e: e.tensor_tensor(out=awsums[0:1, 1:2], in0=small[0:1, 13:14],
                                                in1=awsums[0:1, 0:1], op=OP.max),
                 waits=[gl0, gl1, tok_gm_hold[1]['gm']], inc="v")
        rec = op("v", lambda e: e.reciprocal(out=awsums[0:1, 2:3], in_=awsums[0:1, 1:2]),
                 waits=[gmx], inc="v")
        tc = op("v", lambda e: e.tensor_scalar(out=awsums[0:1, 3:4], in0=awsums[0:1, 2:3],
                                               scalar1=127.0, scalar2=None, op0=OP.mult),
                waits=[rec], inc="v")
        ts1 = op("v", lambda e: e.tensor_scalar(out=awsums[0:1, 5:6], in0=awsums[0:1, 1:2],
                                                scalar1=float(1.0 / 127.0), scalar2=None, op0=OP.mult),
                 waits=[tc], inc="v")
        tsa = op("v", lambda e: e.tensor_tensor(out=awsums[0:1, 4:5], in0=awsums[0:1, 5:6],
                                                in1=small[0:1, 10:11], op=OP.mult),
                 waits=[ts1, tok_alpha], inc="v")
        cst = op("g", lambda e: e.dma_start(out=csa_d[0:1, 0:2], in_=awsums[0:1, 3:5]),
                 waits=[tsa], inc="qg", inc_n=16)
        tok_csa = op("g", lambda e: e.dma_start(out=csa[:, 0:2],
                                                in_=csa_d.ap()[0:1, 0:2].partition_broadcast(P)),
                     waits=[cst], inc="qg", inc_n=16)

        A_end_v = now("v")

        tok_bx = {}
        tok_n2 = {}
        tok_trx = {}
        tok_evx = {}
        tok_sstop = {}
        tok_sev = {}
        tok_ri = {}
        tok_rb = {}
        tok_trr = {}
        tok_evr = {}
        tok_fstop = {}
        tok_g = {}
        tok_sg = {}
        tok_add = {}
        tok_ost = {}

        def bk(t):
            return t // NT, t % NT

        def b_ldx(t):
            b, k = bk(t)
            tok_bx[t] = op("sync", lambda e, b=b, k=k: e.dma_start(
                out=xg[t % 2][:], in_=x_in.ap()[b][k * P:(k + 1) * P, :]),
                waits=[tok_add.get(t - 2) if t >= 2 else tok_txn.get((1, NT - 1)),
                       (tok_cast.get((1, NT - 1)) if t < 2 else None),
                       tok_bx.get(t - 2) if t >= 2 else tok_x.get((1, NT - 1))],
                inc="qs", inc_n=16)

        def b_ldn2(t):
            b, k = bk(t)
            tok_n2[t] = op("s", lambda e, b=b, k=k, t=t: e.dma_start(
                out=xn2_t[t % 2][:], in_=xn2_d.ap()[b][k * P:(k + 1) * P, :]),
                waits=[tok_ri.get(t - 2) if t >= 2 else tok_gst.get((1, NT - 2 + t)),
                       (tok_am.get((1, NT - 2 + t)) if t < 2 else None),
                       tok_n2.get(t - 2) if t >= 2 else tok_gst.get((1, NT - 1))],
                inc="qa", inc_n=16)

        def b_trx(t):
            b, k = bk(t)
            boff = b * NT * D
            for j in range(NCH):
                tok_trx[t] = op("t", lambda e, j=j, boff=boff, k=k, t=t: e.transpose(
                    pT0[t % 2][:, j * P:(j + 1) * P],
                    xnT[:, boff + k * D + j * P: boff + k * D + (j + 1) * P], identb[:]),
                    waits=[(tok_evr.get(t - 2) if t >= 2 else tok_evT[NCH - 1]) if j == 0 else None,
                           (A_end_v if (t == 0 and j == 0) else None),
                           (tok_mev.get((1, NCH - 1)) if (t == 0 and j == 0) else None)],
                    inc="t" if j == NCH - 1 else None)

        def b_evx(t):
            tok_evx[t] = op("s", lambda e, t=t: e.copy(out=xnf_t[t % 2][:], in_=pT0[t % 2][:]),
                            waits=[tok_trx[t],
                                   tok_sstop.get(t - 2) if t >= 2 else None,
                                   (setup_v if t == 0 else None)], inc="s")

        def b_s(t):
            b, k = bk(t)
            msrc = gB if b == 0 else gC
            pm = pG[t % 2]
            for j in range(NCH):
                for h in range(2):
                    last = (j == NCH - 1 and h == 1)
                    tok = op("t", lambda e, j=j, h=h, t=t, msrc=msrc, pm=pm: e.matmul(
                        pm[:, h * 512:(h + 1) * 512],
                        xnf_t[t % 2][:, j * P:(j + 1) * P],
                        msrc[:, j * D + h * 512: j * D + (h + 1) * 512],
                        start=(j == 0), stop=(j == NCH - 1)),
                        waits=[tok_evx[t] if (j == 0 and h == 0) else None,
                               (tok_sev.get(t - 2) if (j == 0 and h == 0 and t >= 2) else None),
                               ((lambda b=b: tok_mev[(b, NCH - 1)]) if (j == 0 and h == 0 and k == 0) else None)],
                        inc="t" if last else None)
                    if last:
                        tok_sstop[t] = tok

        def b_sev(t):
            tok_sev[t] = op("s", lambda e, t=t: e.copy(
                out=s_sb[:], in_=pG[t % 2][:]),
                waits=[tok_sstop[t], tok_sg.get(t - 1)], inc="s")

        def b_trr(t):
            for j in range(NCH):
                tok_trr[t] = op("t", lambda e, j=j, t=t: e.transpose(
                    pT0[t % 2][:, j * P:(j + 1) * P], r_bf[:, j * P:(j + 1) * P], identb[:]),
                    waits=[(tok_evx[t] if j == 0 else None),
                           (tok_rb[t] if j == 0 else None)],
                    inc="t" if j == NCH - 1 else None)

        def b_evr(t):
            tok_evr[t] = op("s", lambda e, t=t: e.copy(out=rt_t[t % 2][:], in_=pT0[t % 2][:]),
                            waits=[tok_trr[t],
                                   tok_fstop.get(t - 2) if t >= 2 else None], inc="s")

        def b_f(t):
            for j in range(NCH):
                for h in range(2):
                    last = (j == NCH - 1 and h == 1)
                    tok = op("t", lambda e, j=j, h=h, t=t: e.matmul(
                        pC[:, h * 512:(h + 1) * 512],
                        rt_t[t % 2][:, j * P:(j + 1) * P],
                        tt_bf[:, j * D + h * 512: j * D + (h + 1) * 512],
                        start=(j == 0), stop=(j == NCH - 1)),
                        waits=[tok_evr[t] if (j == 0 and h == 0) else None,
                               (tok_g.get(t - 1) if (j == 0 and h == 0 and t >= 1) else None),
                               (gmx if (j == 0 and h == 0 and t == 0) else None)],
                        inc="t" if last else None)
                    if last:
                        tok_fstop[t] = tok

        def b_silu(t):
            tok_g[t] = op("s", lambda e, t=t: e.activation(
                out=g_t[t % 2][:], in_=pC[:], func=AF.Silu, scale=csa[:, 1:2]),
                waits=[tok_fstop[t], (tok_csa if t == 0 else None),
                       tok_add.get(t - 2) if t >= 2 else (setup_s if t < 2 else None)],
                inc="s")

        def b_ri(t):
            tok_ri[t] = op("v", lambda e, t=t: e.tensor_scalar(
                out=rint_t[:], in0=xn2_t[t % 2][:], scalar1=csa[:, 0:1], scalar2=None, op0=OP.mult),
                waits=[tok_n2[t], (tok_csa if t == 0 else None),
                       tok_rb.get(t - 1)],
                inc="v")
            tok_rb[t] = op("v", lambda e, t=t: e.tensor_copy(
                out=r_bf[:], in_=rint_t[:]),
                waits=[tok_ri[t], tok_trr.get(t - 1) if t >= 1 else setup_t,
                       (setup_v if t < 2 else None)],
                inc="v")

        def b_sg(t):
            tok_sg[t] = op("v", lambda e, t=t: e.tensor_tensor(
                out=g_t[t % 2][:], in0=s_sb[:], in1=g_t[t % 2][:], op=OP.mult),
                waits=[tok_g[t], tok_sev[t]], inc="v")
            tok_add[t] = op("g", lambda e, t=t: e.tensor_tensor(
                out=(gb_g, gb_b)[t % 2][:], in0=g_t[t % 2][:], in1=xg[t % 2][:], op=OP.add),
                waits=[tok_sg[t], tok_bx[t],
                       tok_ost.get(t - 2) if t >= 2 else tok_am.get((1, NT - 1)),
                       (tok_xn2.get((1, NT - 2 + t)) if t < 2 else None),
                       (tok_u.get((1, NT - 1)) if t < 2 else None)],
                inc="g")

        def b_ost(t):
            b, k = bk(t)
            tok_ost[t] = op("sync", lambda e, b=b, k=k, t=t: e.dma_start(
                out=out_d.ap()[b][k * P:(k + 1) * P, :], in_=(gb_g, gb_b)[t % 2][:]),
                waits=[tok_add[t]], inc="qo", inc_n=16)

        for i in range(-1, NTT + 2):
            if i == -1:
                b_ldn2(0)
                b_ldn2(1)
                b_ri(0)
                b_ldn2(2)
                b_ldx(0)
                b_trx(0)
                b_evx(0)
                continue
            if 0 <= i + 1 < NTT:
                b_trx(i + 1)
                b_evx(i + 1)
            if 0 <= i < NTT:
                b_s(i)
                b_trr(i)
            if 1 <= i + 1 < NTT:
                b_ri(i + 1)
            if 0 <= i + 3 < NTT:
                b_ldn2(i + 3)
            if 0 <= i < NTT:
                b_evr(i)
            if 0 <= i - 1 < NTT:
                b_f(i - 1)
                b_silu(i - 1)
                b_sg(i - 1)
                b_ost(i - 1)
            if 0 <= i < NTT:
                b_sev(i)
            if 0 <= i + 1 < NTT:
                b_ldx(i + 1)

        SCHED["sync"].append((tuple(("qs" + str(i), CTR["qs" + str(i)]) for i in range(DMA_FAM["qs"]))
                              + tuple(("qo" + str(i), CTR["qo" + str(i)]) for i in range(DMA_FAM["qo"])),
                              None, None, 0, 0))
        SCHED["g"].append(((("qg0", CTR["qg0"]),), None, None, 0, 0))
        SCHED["s"].append((tuple(("qa" + str(i), CTR["qa" + str(i)]) for i in range(DMA_FAM["qa"])),
                           None, None, 0, 0))

        def replay(eng_obj, name):
            waited = {}
            for waits, fn, incsem, incval, inc_n in SCHED[name]:
                for w in waits:
                    if callable(w):
                        w = w()
                    if w is None:
                        continue
                    sem, val = w
                    if val > waited.get(sem, 0):
                        eng_obj.wait_ge(SEM[sem], val)
                        waited[sem] = val
                if fn is None:
                    continue
                instr = fn(eng_obj)
                if incsem is not None:
                    instr.then_inc(SEM[incsem], inc_n)

        @block.sync
        def _(sync):
            replay(sync, "sync")

        @block.vector
        def _(vector):
            replay(vector, "v")

        @block.scalar
        def _(scalar):
            replay(scalar, "s")

        @block.tensor
        def _(tensor):
            replay(tensor, "t")

        @block.gpsimd
        def _(gpsimd):
            replay(gpsimd, "g")

    return nc


_cache = {}


def kernel(L0_attn_hidden_states, L0_attn_W_q, L0_ffn_weight, L0_ffn_gamma, L0_ffn_beta):
    x = np.ascontiguousarray(L0_attn_hidden_states, dtype=np.float32)
    wq = np.ascontiguousarray(L0_attn_W_q, dtype=np.float32)
    w = np.ascontiguousarray(L0_ffn_weight, dtype=np.float32)
    gamma = np.ascontiguousarray(L0_ffn_gamma, dtype=np.float32).reshape(1, D)
    beta = np.ascontiguousarray(L0_ffn_beta, dtype=np.float32).reshape(1, D)

    if "nc" not in _cache:
        _cache["nc"] = build()
    nc = _cache["nc"]

    in_maps = []
    for c in range(N_CORES):
        in_maps.append({
            "x_in": x[c * NB:(c + 1) * NB],
            "wq_in": wq,
            "w_in": w,
            "gamma_in": gamma,
            "beta_in": beta,
        })
    res = run_bass_kernel_spmd(nc, in_maps, core_ids=list(range(N_CORES)))
    out = np.concatenate([np.asarray(r["out_d"]) for r in res.results], axis=0)
    return out


# version 21 (final)
# speedup vs baseline: 6.9620x; 6.9620x over previous
import numpy as np
import ml_dtypes
import concourse.bass as bass
import concourse.mybir as mybir
import concourse.bass_utils as _bass_utils
from concourse.bass_utils import run_bass_kernel_spmd
from contextlib import ExitStack

if not getattr(_bass_utils, "_ldwopt_patched", False):
    _orig_run_command = _bass_utils.run_command

    def _run_command_ldwopt(cmd, *a, **kw):
        try:
            cmd = ["--enable-ldw-opt=true" if c == "--enable-ldw-opt=false" else c for c in cmd]
        except TypeError:
            pass
        return _orig_run_command(cmd, *a, **kw)

    _bass_utils.run_command = _run_command_ldwopt
    _bass_utils._ldwopt_patched = True

F32 = mybir.dt.float32
BF16 = mybir.dt.bfloat16
I32 = mybir.dt.int32
I8 = mybir.dt.int8
AF = mybir.ActivationFunctionType
OP = mybir.AluOpType
AX = mybir.AxisListType.X

P = 128
D = 1024
SEQ = 2048
NB = 2
NT = 16
NTT = NB * NT
NCH = 8
EPS = 1e-5
N_CORES = 8


def build():
    nc = bass.Bass("TRN2", target_bir_lowering=False)

    x_in = nc.dram_tensor("x_in", [NB, SEQ, D], F32, kind="ExternalInput")
    wq_in = nc.dram_tensor("wq_in", [D, D], F32, kind="ExternalInput")
    w_in = nc.dram_tensor("w_in", [D, D], F32, kind="ExternalInput")
    gamma_in = nc.dram_tensor("gamma_in", [1, D], F32, kind="ExternalInput")
    beta_in = nc.dram_tensor("beta_in", [1, D], F32, kind="ExternalInput")
    out_d = nc.dram_tensor("out_d", [NB, SEQ, D], F32, kind="ExternalOutput")
    xn2_d = nc.dram_tensor("xn2_d", [NB, SEQ, D], F32)
    cc_in0 = nc.dram_tensor("cc_in0", [1, 8], F32)
    cc_out0 = nc.dram_tensor("cc_out0", [1, 8], F32)
    cc_in1 = nc.dram_tensor("cc_in1", [1, 8], F32)
    cc_out1 = nc.dram_tensor("cc_out1", [1, 8], F32)
    delta_d = nc.dram_tensor("delta_d", [1, 1], F32)
    al_d = nc.dram_tensor("al_d", [1, 2], F32)
    csa_d = nc.dram_tensor("csa_d", [1, 2], F32)
    identb_d = nc.inline_tensor(np.eye(P, dtype=ml_dtypes.bfloat16), name="identb")
    identf_d = nc.inline_tensor(np.eye(P, dtype=np.float32), name="identf")
    eps_dd = nc.inline_tensor(
        np.concatenate([np.full((P, 1), EPS, np.float32),
                        np.full((P, 1), 1.0 / D, np.float32),
                        np.full((P, 1), EPS * EPS, np.float32)], axis=1), name="epsc")

    SCHED = {"sync": [], "v": [], "s": [], "t": [], "g": []}
    DMA_FAM = {"qs": 4, "qo": 2, "qa": 4, "qg": 2}
    SEM_NAMES = ["v", "s", "t", "g", "cc"] + [f + str(i) for f, n in DMA_FAM.items() for i in range(n)]
    CTR = {n: 0 for n in SEM_NAMES}
    FAM_ROT = {f: 0 for f in DMA_FAM}

    def op(eng, fn, waits=(), inc=None, inc_n=1):
        waits = [w for w in waits if w is not None]
        assert all(callable(w) or (isinstance(w, tuple) and len(w) == 2) for w in waits)
        if inc in DMA_FAM:
            fam = inc
            inc = fam + str(FAM_ROT[fam] % DMA_FAM[fam])
            FAM_ROT[fam] += 1
            waits.append((inc, CTR[inc]))
        tok = None
        if inc is not None:
            CTR[inc] += inc_n
            tok = (inc, CTR[inc])
        SCHED[eng].append((tuple(waits), fn, inc, CTR[inc] if inc else 0, inc_n))
        return tok

    def now(sem):
        return (sem, CTR[sem])

    es = ExitStack()
    with es:
        S_ = lambda nm, shp, dt=F32: es.enter_context(nc.sbuf_tensor(nm, shp, dt))
        xnT = S_("xnT", [P, NTT * D], BF16)
        gA = S_("gA", [P, NCH * D], BF16)
        gB = S_("gB", [P, NCH * D], BF16)
        gC = S_("gC", [P, NCH * D], BF16)
        wq_bf = S_("wq_bf", [P, NCH * D], BF16)
        tt_bf = S_("tt_bf", [P, NCH * D], BF16)
        gb_g = S_("gb_g", [P, D])
        gb_b = S_("gb_b", [P, D])
        xg = [S_(f"xg{i}", [P, D]) for i in range(3)]
        xn_t = [S_(f"xn_t{i}", [P, D]) for i in range(3)]
        xn2_t = [S_("xn2a", [P, D]), S_("xn2b", [P, D])]
        g_t = [S_("g0", [P, D]), S_("g1", [P, D])]
        rint_t = S_("rint", [P, D], I8)
        r_bf = S_("rbf", [P, D], BF16)
        xnf_t = [S_("xnf0", [P, D], BF16), S_("xnf1", [P, D], BF16)]
        rt_t = [S_("rt0", [P, D], BF16), S_("rt1", [P, D], BF16)]
        s_sb = S_("ssb", [P, D], BF16)
        bn6 = [S_("bn6", [P, 12])] * 2
        stt = S_("stt", [P, 48])
        amall = S_("amall", [P, 32])
        awsums = S_("awsums", [P, 8])
        nums = S_("nums", [P, 8])
        dens = S_("dens", [P, 8])
        small = S_("small", [P, 14])
        csa = S_("csa", [P, 2])
        delta_t = S_("delta_t", [P, 4])
        identb = S_("identb_s", [P, P], BF16)
        identf = S_("identf_s", [P, P])

        pA = es.enter_context(nc.psum_tensor("pA", [P, D], F32))
        pB = es.enter_context(nc.psum_tensor("pB", [P, D], F32))
        pC = es.enter_context(nc.psum_tensor("pC", [P, D], F32))
        pT0 = [es.enter_context(nc.psum_tensor("pT0a", [P, D], BF16)),
               es.enter_context(nc.psum_tensor("pT0b", [P, D], BF16))]
        pG = [pA, pB, pC]

        SEM = {}
        for s in SEM_NAMES:
            SEM[s] = es.enter_context(nc.semaphore("sem_" + s))
        block = es.enter_context(nc.Block())

        op("g", lambda e: e.dma_start(out=identb[:], in_=identb_d[:]), inc="qg", inc_n=16)
        op("g", lambda e: e.dma_start(out=identf[:], in_=identf_d[:]), inc="qg", inc_n=16)
        op("g", lambda e: e.dma_start(out=delta_t[:, 1:4], in_=eps_dd[:]), inc="qg", inc_n=16)
        op("g", lambda e: e.dma_start(out=gb_g[:], in_=gamma_in.ap().partition_broadcast(P)), inc="qg", inc_n=16)
        op("g", lambda e: e.dma_start(out=gb_b[:], in_=beta_in.ap().partition_broadcast(P)), inc="qg", inc_n=16)
        tok_consts = (now("qg0"), now("qg1"))

        tok_absw = [None] * NCH
        tok_wld = [None] * NCH
        for c in range(NCH):
            par = c % 2
            tok_wld[c] = op("sync", lambda e, c=c, par=par: e.dma_start(
                out=g_t[par][:], in_=w_in.ap()[c * P:(c + 1) * P, :]),
                waits=[tok_absw[c - 2] if c >= 2 else None], inc="qs", inc_n=16)
            tok_absw[c] = op("s", lambda e, c=c, par=par: e.activation(
                out=xn_t[0][:], in_=g_t[par][:], func=AF.Abs, accum_out=awsums[:, c:c + 1]),
                waits=[tok_wld[c], (tok_absw[c - 1] if c >= 1 else None)], inc="s")
        tok_rsum = op("v", lambda e: e.reduce_sum(out=small[:, 0:1], in_=awsums[:, 0:8], axis=AX),
                      waits=[tok_absw[NCH - 1]], inc="v")
        tok_trs = op("t", lambda e: e.transpose(pA[0:1, 0:P], small[:, 0:1], identf[:]),
                     waits=[tok_rsum, tok_consts[0], tok_consts[1]], inc="t")
        tok_tot = op("v", lambda e: e.reduce_sum(out=small[0:1, 1:2], in_=pA[0:1, 0:P], axis=AX),
                     waits=[tok_trs], inc="v")
        tok_del = op("v", lambda e: e.tensor_scalar(out=small[0:1, 2:3], in0=small[0:1, 1:2],
                                                    scalar1=float(0.7 / (D * D)), scalar2=None, op0=OP.mult),
                     waits=[tok_tot], inc="v")
        tok_dst = op("g", lambda e: e.dma_start(out=delta_d[0:1, 0:1], in_=small[0:1, 2:3]),
                     waits=[tok_del], inc="qg", inc_n=16)
        tok_dbk = op("g", lambda e: e.dma_start(out=delta_t[:, 0:1], in_=delta_d.ap()[0:1, 0:1].partition_broadcast(P)),
                     waits=[tok_dst], inc="qg", inc_n=16)

        tok_T = [None] * NCH
        tok_trT = [None] * NCH
        tok_evT = [None] * NCH
        tok_num = [None] * NCH
        tok_sgn = [None] * NCH
        tok_mkl = [None] * NCH
        tok_wld2 = [None] * NCH
        absw_b = [xn_t[0], xn_t[2]]
        sgn_b = [r_bf, rt_t[1]]

        def p2_s0(c):
            par = c % 2
            tok_wld2[c] = op("sync", lambda e, c=c, par=par: e.dma_start(
                out=g_t[par][:], in_=w_in.ap()[c * P:(c + 1) * P, :]),
                waits=[tok_sgn[c - 2] if c >= 2 else tok_absw[NCH - 1],
                       tok_mkl[c - 2] if c >= 2 else None],
                inc="qs", inc_n=16)

        def p2_s1(c):
            par = c % 2
            op("s", lambda e, c=c, par=par: e.activation(
                out=absw_b[par][:], in_=g_t[par][:], func=AF.Abs),
                waits=[tok_wld2[c], tok_num[c - 2] if c >= 2 else tok_rsum], inc="s")
            tok_sgn[c] = op("s", lambda e, c=c, par=par: e.activation(
                out=sgn_b[par][:], in_=g_t[par][:], func=AF.Sign),
                waits=[tok_T[c - 2] if c >= 2 else None], inc="s")

        def p2_s2(c):
            par = c % 2
            tok_mkl[c] = op("v", lambda e, c=c, par=par: e.tensor_scalar(
                out=xnf_t[0][:], in0=absw_b[par][:], scalar1=delta_t[:, 0:1], scalar2=0.0,
                op0=OP.is_gt, op1=OP.add, accum_out=dens[:, c:c + 1]),
                waits=[tok_sgn[c], tok_dbk, tok_T[c - 1] if c >= 1 else None], inc="v")
            tok_num[c] = op("v", lambda e, c=c, par=par: e.scalar_tensor_tensor(
                out=xn_t[1][:], in0=absw_b[par][:], scalar=1.0, in1=xnf_t[0][:],
                op0=OP.mult, op1=OP.mult, accum_out=nums[:, c:c + 1]),
                waits=[tok_mkl[c]], inc="v")
            tok_T[c] = op("v", lambda e, c=c, par=par: e.tensor_tensor(
                out=rt_t[0][:], in0=sgn_b[par][:], in1=xnf_t[0][:], op=OP.mult),
                waits=[tok_num[c], tok_trT[c - 1] if c >= 1 else None], inc="v")

        def p2_s3(c):
            for j in range(NCH):
                tok_trT[c] = op("t", lambda e, j=j: e.transpose(
                    pT0[0][:, j * P:(j + 1) * P], rt_t[0][:, j * P:(j + 1) * P], identb[:]),
                    waits=[tok_T[c] if j == 0 else None,
                           tok_evT[c - 1] if (j == 0 and c >= 1) else None,
                           (tok_trs if (j == 0 and c == 0) else None)],
                    inc="t" if j == NCH - 1 else None)
            tok_evT[c] = op("s", lambda e, c=c: e.copy(
                out=tt_bf[:].rearrange("p (j o) -> p j o", o=D)[:, :, c * P:(c + 1) * P],
                in_=pT0[0][:].rearrange("p (j o) -> p j o", o=P)),
                waits=[tok_trT[c]], inc="s")

        for i in range(NCH + 3):
            if i < NCH:
                p2_s0(i)
            if 0 <= i - 1 < NCH:
                p2_s1(i - 1)
            if 0 <= i - 3 < NCH:
                p2_s3(i - 3)
            if 0 <= i - 2 < NCH:
                p2_s2(i - 2)

        tok_ns = op("v", lambda e: e.reduce_sum(out=small[:, 3:4], in_=nums[:, 0:8], axis=AX),
                    waits=[tok_num[NCH - 1]], inc="v")
        tok_ds = op("v", lambda e: e.reduce_sum(out=small[:, 4:5], in_=dens[:, 0:8], axis=AX),
                    waits=[tok_ns], inc="v")
        tok_tr2 = op("t", lambda e: e.transpose(pA[0:2, 0:P], small[:, 3:5], identf[:]),
                     waits=[tok_ds], inc="t")
        tok_nd = op("v", lambda e: e.reduce_sum(out=small[0:2, 5:6], in_=pA[0:2, 0:P], axis=AX),
                    waits=[tok_tr2], inc="v")
        tok_alst = op("g", lambda e: e.dma_start(out=al_d[0:1, 0:2], in_=small[0:2, 5:6]),
                      waits=[tok_nd], inc="qg", inc_n=16)
        tok_albk = op("g", lambda e: e.dma_start(out=small[0:1, 6:8], in_=al_d[0:1, 0:2]),
                      waits=[tok_alst], inc="qg", inc_n=16)
        tok_dmx = op("v", lambda e: e.tensor_scalar(out=small[0:1, 8:9], in0=small[0:1, 7:8],
                                                    scalar1=1.0, scalar2=None, op0=OP.max),
                     waits=[tok_albk], inc="v")
        tok_rd = op("v", lambda e: e.reciprocal(out=small[0:1, 9:10], in_=small[0:1, 8:9]),
                    waits=[tok_dmx], inc="v")
        tok_alpha = op("v", lambda e: e.tensor_tensor(out=small[0:1, 10:11], in0=small[0:1, 6:7],
                                                      in1=small[0:1, 9:10], op=OP.mult),
                       waits=[tok_rd], inc="v")

        tok_wqc = [None] * NCH
        tok_wqc[NCH - 1] = op("g", lambda e: e.dma_start(
            out=wq_bf[:].rearrange("p (c i) -> p c i", i=D),
            in_=wq_in.ap().rearrange("(c p) i -> p c i", p=P)),
            inc="qg", inc_n=16)
        tok_wql = tok_wqc[NCH - 1]

        setup_v, setup_s, setup_t = now("v"), now("s"), now("t")

        tok_x = {}
        tok_aggrD = {}
        tok_ti = {}
        tok_lv2 = {}
        tok_diff = {}
        tok_txn = {}
        tok_cast = {}
        tok_r2D = {}
        tok_u = {}
        tok_mul = {}
        tok_xn2 = {}
        tok_am = {}
        tok_gst = {}
        tok_gstop = {}
        tok_gev = {}
        tok_mstop = {}
        tok_mev = {}
        tok_ccgo = {}
        tok_ccred = {}
        tok_gm_hold = {}

        def stC(k, lo, hi):
            return stt[:, (k % 4) * 12 + lo:(k % 4) * 12 + hi]

        def a_load(b, k):
            par = (b * NT + k) % 3
            tok_x[(b, k)] = op("sync", lambda e, b=b, k=k, par=par: e.dma_start(
                out=xg[par][:], in_=x_in.ap()[b][k * P:(k + 1) * P, :]),
                waits=[tok_txn.get((b, k - 3)) if k >= 3 else (tok_txn.get((b - 1, NT - 3 + k)) if b else setup_v),
                       tok_cast.get((b, k - 3)) if k >= 3 else (tok_cast.get((b - 1, NT - 3 + k)) if b else setup_s),
                       tok_x.get((b, k - 3)) if k >= 3 else (tok_x.get((b - 1, NT - 3 + k)) if b else None)],
                inc="qs", inc_n=16)

        def a_s1(b, k):
            par = k % 2
            t1 = op("v", lambda e, par=par, b=b, k=k: e.bn_stats(out=bn6[par][:, 0:6], in_=xg[(b * NT + k) % 3][:, 0:512]),
                    waits=[tok_x[(b, k)],
                           tok_aggrD.get((b, k - 1)) or (tok_aggrD.get((b - 1, NT - 1)) if b else None)], inc="v")
            t2 = op("v", lambda e, par=par, b=b, k=k: e.bn_stats(out=bn6[par][:, 6:12], in_=xg[(b * NT + k) % 3][:, 512:1024]),
                    waits=[], inc="v")
            tok_aggrD[(b, k)] = op("v", lambda e, par=par, k=k: e.bn_aggr(out=stC(k, 0, 2), in_=bn6[par][:, 0:12]),
                                   waits=[t2, tok_r2D.get((b, k - 4)) or (tok_r2D.get((b - 1, NT - 4 + k)) if b else None)],
                                   inc="v")

        def a_s2(b, k):
            tl = op("s", lambda e, k=k: e.activation(out=stC(k, 2, 3), in_=stC(k, 1, 2),
                                                     func=AF.Ln, bias=delta_t[:, 1:2]),
                    waits=[tok_aggrD[(b, k)],
                           tok_diff.get((b, k - 4)) or (tok_diff.get((b - 1, NT - 4 + k)) if b else None)], inc="s")
            tok_ti[(b, k)] = op("s", lambda e, k=k: e.activation(out=stC(k, 3, 4), in_=stC(k, 2, 3),
                                                                 func=AF.Exp, scale=-0.5),
                                waits=[tl], inc="s")
            tok_lv2[(b, k)] = op("s", lambda e, k=k: e.activation(out=stC(k, 4, 5), in_=stC(k, 1, 2),
                                                                  func=AF.Ln, scale=float(1.0 + EPS),
                                                                  bias=delta_t[:, 3:4]),
                                waits=[tok_ti[(b, k)]], inc="s")

        def a_s3(b, k):
            tok_txn[(b, k)] = op("v", lambda e, b=b, k=k: e.tensor_scalar(
                out=xn_t[(b * NT + k) % 3][:], in0=xg[(b * NT + k) % 3][:], scalar1=stC(k, 0, 1), scalar2=stC(k, 3, 4),
                op0=OP.subtract, op1=OP.mult),
                waits=[tok_ti[(b, k)],
                       tok_u.get((b, k - 3)) or (tok_u.get((b - 1, NT - 3 + k)) if b else None),
                       setup_s if (b == 0 and k == 0) else None], inc="v")
            tok_diff[(b, k)] = op("v", lambda e, k=k: e.tensor_tensor(
                out=stC(k, 5, 6), in0=stC(k, 2, 3), in1=stC(k, 4, 5), op=OP.subtract),
                waits=[tok_lv2[(b, k)],
                       tok_r2D.get((b, k - 4)) or (tok_r2D.get((b - 1, NT - 4 + k)) if b else None)], inc="v")

        def a_s4(b, k):
            boff = b * NT * D
            tok_cast[(b, k)] = op("s", lambda e, b=b, k=k, boff=boff: e.activation(
                out=xnT[:, boff + k * D: boff + (k + 1) * D], in_=xn_t[(b * NT + k) % 3][:],
                func=AF.Copy),
                waits=[tok_txn[(b, k)]], inc="s")
            tok_r2D[(b, k)] = op("s", lambda e, k=k: e.activation(
                out=stC(k, 6, 7), in_=stC(k, 5, 6), func=AF.Exp, scale=0.5),
                waits=[tok_cast[(b, k)], tok_diff[(b, k)],
                       tok_u.get((b, k - 4)) or (tok_u.get((b - 1, NT - 4 + k)) if b else None)],
                inc="s")

        def a_s7(b, k):
            par = k % 2
            tok_u[(b, k)] = op("v", lambda e, b=b, k=k, par=par: e.scalar_tensor_tensor(
                out=xn2_t[par][:], in0=xn_t[(b * NT + k) % 3][:], scalar=stC(k, 6, 7),
                in1=gb_g[:], op0=OP.mult, op1=OP.mult),
                waits=[tok_r2D[(b, k)],
                       tok_gst.get((b, k - 2)) if k >= 2 else (tok_gst.get((b - 1, NT - 2 + k)) if b else None),
                       tok_consts[0] if (b == 0 and k == 0) else None,
                       tok_consts[1] if (b == 0 and k == 0) else None], inc="v")
            tok_xn2[(b, k)] = op("g", lambda e, par=par: e.tensor_tensor(
                out=xn2_t[par][:], in0=xn2_t[par][:], in1=gb_b[:], op=OP.add),
                waits=[tok_u[(b, k)]], inc="g")
        def a_s8(b, k):
            par = k % 2
            tok_am[(b, k)] = op("v", lambda e, par=par, b=b, k=k: e.tensor_reduce(
                out=amall[:, b * NT + k: b * NT + k + 1], in_=xn2_t[par][:], axis=AX,
                op=OP.max, apply_absolute_value=True),
                waits=[tok_xn2[(b, k)]], inc="v")
            tok_gst[(b, k)] = op("s", lambda e, b=b, k=k, par=par: e.dma_start(
                out=xn2_d.ap()[b][k * P:(k + 1) * P, :], in_=xn2_t[par][:]),
                waits=[tok_am[(b, k)]], inc="qa", inc_n=16)

        def a_pe(b, k):
            boff = b * NT * D
            for c in range(3):
                for h in range(2):
                    last = (k == NT - 1 and h == 1)
                    tok = op("t", lambda e, b=b, k=k, c=c, h=h, boff=boff: e.matmul(
                        pG[c][:, h * 512:(h + 1) * 512],
                        xnT[:, boff + k * D + c * P: boff + k * D + (c + 1) * P],
                        xnT[:, boff + k * D + h * 512: boff + k * D + (h + 1) * 512],
                        start=(k == 0), stop=(k == NT - 1)),
                        waits=[tok_cast[(b, k)] if (c == 0 and h == 0) else None,
                               (tok_gev.get((b - 1, 7)) if (b and k == 0 and c == 0 and h == 0) else None),
                               (tok_mev.get((b - 1, NCH - 1)) if (b and k == 0 and c == 0 and h == 0) else None),
                               (tok_ccred.get(b - 1) if (b and k == 0 and c == 0 and h == 0) else None),
                               (setup_t if (b == 0 and k == 0 and c == 0 and h == 0) else None)],
                        inc="t" if last else None)
                    if last:
                        tok_gstop[(b, c)] = tok

        def phaseA(b, defer=None):
            defer = defer or []
            for i in range(NT + 5):
                if i < NT:
                    a_load(b, i)
                if 0 <= i - 1 < NT:
                    a_s1(b, i - 1)
                    a_s2(b, i - 1)
                if 0 <= i - 2 < NT:
                    a_s3(b, i - 2)
                    a_s4(b, i - 2)
                    if b == 0:
                        a_pe(b, i - 2)
                if 0 <= i - 4 < NT:
                    a_s7(b, i - 4)
                if 0 <= i - 5 < NT:
                    a_s8(b, i - 5)
                if i >= 1 and defer:
                    defer.pop(0)()
            while defer:
                defer.pop(0)()

        def g_rest_m_cc(b, defer=None):
            boff = b * NT * D
            gdst = gA
            mdst = gB if b == 0 else gC
            ccin = cc_in0 if b == 0 else cc_in1
            ccout = cc_out0 if b == 0 else cc_out1
            DQ = defer if defer is not None else []
            def emit(fn):
                if defer is None:
                    fn()
                else:
                    DQ.append(fn)
            if b == 0:
                for c in range(3):
                    emit(lambda c=c: tok_gev.__setitem__((b, c), op("s", lambda e, c=c, gdst=gdst: e.copy(
                        out=gdst[:, c * D:(c + 1) * D], in_=pG[c][:]),
                        waits=[tok_gstop[(b, c)]], inc="s")))
            hold = {}
            def _amr():
                hold['amr'] = op("v", lambda e, b=b: e.reduce_max(out=small[:, 11:12],
                                                                  in_=amall[:, b * NT:(b + 1) * NT], axis=AX),
                                 waits=[tok_am[(b, NT - 1)],
                                        (tok_gm_hold[0]['amt'] if b == 1 else None)], inc="v")
            emit(_amr)
            if b == 0:
                hold['amt'] = op("t", lambda e, b=b: e.transpose(pC[0:1, 0:P], small[:, 11:12], identf[:]),
                                 waits=[lambda: hold['amr'], lambda: tok_gev[(0, 2)]], inc="t")

            def _gm():
                hold['gm'] = op("v", lambda e, b=b: e.reduce_max(out=small[0:1, 12:13],
                                                                 in_=pC[0:1, 0:P], axis=AX),
                                waits=[lambda: hold['amt'],
                                       (tok_ccred[0] if b == 1 else None)], inc="v")
                tok_ccred[b] = op("v", lambda e, b=b: e.tensor_copy(
                    out=nums[0:1, 0:8],
                    in_=small[0:1, 12:13].broadcast_to((1, 8))),
                    waits=[hold['gm'], tok_ccgo.get(b - 1), (tok_ns if b == 0 else None)], inc="v")
                cci = op("g", lambda e, ccin=ccin: e.dma_start(out=ccin[0:1, 0:8], in_=nums[0:1, 0:8]),
                         waits=[tok_ccred[b]], inc="qg", inc_n=16)
                tok_ccgo[b] = op("g", lambda e, ccin=ccin, ccout=ccout: e.collective_compute(
                    "AllReduce", OP.max, replica_groups=[list(range(N_CORES))],
                    ins=[ccin.ap().opt()], outs=[ccout.ap().opt()]),
                    waits=[cci], inc="cc", inc_n=1)
            emit(_gm)
            tok_gm_hold[b] = hold
            rem = list(range(3, NCH)) if b == 0 else list(range(NCH))
            BANKS = [0, 1, 0, 1, 2, 0, 1, 2, 0, 1, 2, 0, 1, 2, 0, 1]
            first_pc = BANKS.index(2)
            for bi, c in enumerate(rem):
                if bi == first_pc and b == 1:
                    hold['amt'] = op("t", lambda e, b=b: e.transpose(pC[0:1, 0:P], small[:, 11:12], identf[:]),
                                     waits=[lambda: hold['amr'], lambda: tok_mev[(0, NCH - 1)]],
                                     inc="t")
                pm = pG[BANKS[bi]]
                previ = None
                for q in range(bi - 1, -1, -1):
                    if BANKS[q] == BANKS[bi]:
                        previ = q
                        break
                if previ is not None:
                    prev_user = (lambda b=b, c=rem[previ]: tok_gev[(b, c)])
                elif b == 0:
                    prev_user = (lambda bk=BANKS[bi]: tok_gev[(0, bk)])
                else:
                    prev_user = (lambda: tok_mev[(0, NCH - 1)])
                for k in range(NT):
                    for h in range(2):
                        last = (k == NT - 1 and h == 1)
                        tok = op("t", lambda e, k=k, c=c, h=h, pm=pm, boff=boff: e.matmul(
                            pm[:, h * 512:(h + 1) * 512],
                            xnT[:, boff + k * D + c * P: boff + k * D + (c + 1) * P],
                            xnT[:, boff + k * D + h * 512: boff + k * D + (h + 1) * 512],
                            start=(k == 0), stop=(k == NT - 1)),
                            waits=[prev_user if (k == 0 and h == 0) else None,
                                   (hold['amt'] if (k == 0 and h == 0 and bi == first_pc) else None),
                                   ((lambda b=b: tok_gm_hold[b]['gm']) if (k == 0 and h == 0 and bi == first_pc) else None),
                                   (tok_cast[(b, NT - 1)] if (k == 0 and h == 0 and bi == 0 and b == 1) else None)],
                            inc="t" if last else None)
                        if last:
                            tok_gstop[(b, c)] = tok
                emit(lambda c=c, pm=pm: tok_gev.__setitem__((b, c), op(
                    "s", lambda e, c=c, pm=pm, gdst=gdst: e.copy(
                        out=gdst[:, c * D:(c + 1) * D], in_=pm[:]),
                    waits=[tok_gstop[(b, c)]], inc="s")))
            for hc in range(NCH):
                pm = pG[BANKS[(len(rem) + hc) % len(BANKS)]]
                for oc in range(NCH):
                    for h in range(2):
                        last = (oc == NCH - 1 and h == 1)
                        tok = op("t", lambda e, hc=hc, oc=oc, h=h, pm=pm, gdst=gdst: e.matmul(
                            pm[:, h * 512:(h + 1) * 512],
                            wq_bf[:, oc * D + hc * P: oc * D + (hc + 1) * P],
                            gdst[:, oc * D + h * 512: oc * D + (h + 1) * 512],
                            start=(oc == 0), stop=(oc == NCH - 1)),
                            waits=[((lambda b=b: tok_gev[(b, NCH - 1)]) if (hc == 0 and oc == 0 and h == 0) else None),
                                   (tok_wqc[NCH - 1] if (hc == 0 and oc == 0 and h == 0) else None),
                                   ((lambda b=b, hc=hc: tok_gev[(b, NCH - 3 + hc)]) if (hc in (1, 2) and oc == 0 and h == 0) else None),
                                   ((lambda b=b, hc=hc: tok_mev[(b, hc - 3)]) if (hc >= 3 and oc == 0 and h == 0) else None)],
                            inc="t" if last else None)
                        if last:
                            tok_mstop[(b, hc)] = tok
                emit(lambda hc=hc, pm=pm: tok_mev.__setitem__((b, hc), op(
                    "s", lambda e, hc=hc, pm=pm, mdst=mdst: e.copy(
                        out=mdst[:, hc * D:(hc + 1) * D], in_=pm[:]),
                    waits=[tok_mstop[(b, hc)]], inc="s")))

        phaseA(0)
        DQ0 = []
        g_rest_m_cc(0, defer=DQ0)
        phaseA(1, defer=DQ0)
        g_rest_m_cc(1)

        gl0 = op("g", lambda e: e.dma_start(out=small[0:1, 13:14], in_=cc_out0[0:1, 0:1]),
                 waits=[tok_ccgo[0]], inc="qg", inc_n=16)
        gl1 = op("g", lambda e: e.dma_start(out=awsums[0:1, 0:1], in_=cc_out1[0:1, 0:1]),
                 waits=[tok_ccgo[1], tok_ccred[1]], inc="qg", inc_n=16)
        gmx = op("v", lambda e: e.tensor_tensor(out=awsums[0:1, 1:2], in0=small[0:1, 13:14],
                                                in1=awsums[0:1, 0:1], op=OP.max),
                 waits=[gl0, gl1, tok_gm_hold[1]['gm']], inc="v")
        rec = op("v", lambda e: e.reciprocal(out=awsums[0:1, 2:3], in_=awsums[0:1, 1:2]),
                 waits=[gmx], inc="v")
        tc = op("v", lambda e: e.tensor_scalar(out=awsums[0:1, 3:4], in0=awsums[0:1, 2:3],
                                               scalar1=127.0, scalar2=None, op0=OP.mult),
                waits=[rec], inc="v")
        ts1 = op("v", lambda e: e.tensor_scalar(out=awsums[0:1, 5:6], in0=awsums[0:1, 1:2],
                                                scalar1=float(1.0 / 127.0), scalar2=None, op0=OP.mult),
                 waits=[tc], inc="v")
        tsa = op("v", lambda e: e.tensor_tensor(out=awsums[0:1, 4:5], in0=awsums[0:1, 5:6],
                                                in1=small[0:1, 10:11], op=OP.mult),
                 waits=[ts1, tok_alpha], inc="v")
        cst = op("g", lambda e: e.dma_start(out=csa_d[0:1, 0:2], in_=awsums[0:1, 3:5]),
                 waits=[tsa], inc="qg", inc_n=16)
        tok_csa = op("g", lambda e: e.dma_start(out=csa[:, 0:2],
                                                in_=csa_d.ap()[0:1, 0:2].partition_broadcast(P)),
                     waits=[cst], inc="qg", inc_n=16)

        A_end_v = now("v")

        tok_bx = {}
        tok_n2 = {}
        tok_trx = {}
        tok_evx = {}
        tok_sstop = {}
        tok_sev = {}
        tok_ri = {}
        tok_rb = {}
        tok_trr = {}
        tok_evr = {}
        tok_fstop = {}
        tok_g = {}
        tok_sg = {}
        tok_add = {}
        tok_ost = {}

        def bk(t):
            return t // NT, t % NT

        def b_ldx(t):
            b, k = bk(t)
            tok_bx[t] = op("sync", lambda e, b=b, k=k: e.dma_start(
                out=xg[t % 2][:], in_=x_in.ap()[b][k * P:(k + 1) * P, :]),
                waits=[tok_add.get(t - 2) if t >= 2 else tok_txn.get((1, NT - 1)),
                       (tok_cast.get((1, NT - 1)) if t < 2 else None),
                       tok_bx.get(t - 2) if t >= 2 else tok_x.get((1, NT - 1))],
                inc="qs", inc_n=16)

        def b_ldn2(t):
            b, k = bk(t)
            tok_n2[t] = op("s", lambda e, b=b, k=k, t=t: e.dma_start(
                out=xn2_t[t % 2][:], in_=xn2_d.ap()[b][k * P:(k + 1) * P, :]),
                waits=[tok_ri.get(t - 2) if t >= 2 else tok_gst.get((1, NT - 2 + t)),
                       (tok_am.get((1, NT - 2 + t)) if t < 2 else None),
                       tok_n2.get(t - 2) if t >= 2 else tok_gst.get((1, NT - 1))],
                inc="qa", inc_n=16)

        def b_trx(t):
            b, k = bk(t)
            boff = b * NT * D
            for j in range(NCH):
                tok_trx[t] = op("t", lambda e, j=j, boff=boff, k=k, t=t: e.transpose(
                    pT0[t % 2][:, j * P:(j + 1) * P],
                    xnT[:, boff + k * D + j * P: boff + k * D + (j + 1) * P], identb[:]),
                    waits=[(tok_evr.get(t - 2) if t >= 2 else tok_evT[NCH - 1]) if j == 0 else None,
                           (A_end_v if (t == 0 and j == 0) else None),
                           (tok_mev.get((1, NCH - 1)) if (t == 0 and j == 0) else None)],
                    inc="t" if j == NCH - 1 else None)

        def b_evx(t):
            tok_evx[t] = op("s", lambda e, t=t: e.copy(out=xnf_t[t % 2][:], in_=pT0[t % 2][:]),
                            waits=[tok_trx[t],
                                   tok_sstop.get(t - 2) if t >= 2 else None,
                                   (setup_v if t == 0 else None)], inc="s")

        def b_s(t):
            b, k = bk(t)
            msrc = gB if b == 0 else gC
            pm = pG[t % 2]
            for j in range(NCH):
                for h in range(2):
                    last = (j == NCH - 1 and h == 1)
                    tok = op("t", lambda e, j=j, h=h, t=t, msrc=msrc, pm=pm: e.matmul(
                        pm[:, h * 512:(h + 1) * 512],
                        xnf_t[t % 2][:, j * P:(j + 1) * P],
                        msrc[:, j * D + h * 512: j * D + (h + 1) * 512],
                        start=(j == 0), stop=(j == NCH - 1)),
                        waits=[tok_evx[t] if (j == 0 and h == 0) else None,
                               (tok_sev.get(t - 2) if (j == 0 and h == 0 and t >= 2) else None),
                               ((lambda b=b: tok_mev[(b, NCH - 1)]) if (j == 0 and h == 0 and k == 0) else None)],
                        inc="t" if last else None)
                    if last:
                        tok_sstop[t] = tok

        def b_sev(t):
            tok_sev[t] = op("s", lambda e, t=t: e.copy(
                out=s_sb[:], in_=pG[t % 2][:]),
                waits=[tok_sstop[t], tok_sg.get(t - 1)], inc="s")

        def b_trr(t):
            for j in range(NCH):
                tok_trr[t] = op("t", lambda e, j=j, t=t: e.transpose(
                    pT0[t % 2][:, j * P:(j + 1) * P], r_bf[:, j * P:(j + 1) * P], identb[:]),
                    waits=[(tok_evx[t] if j == 0 else None),
                           (tok_rb[t] if j == 0 else None)],
                    inc="t" if j == NCH - 1 else None)

        def b_evr(t):
            tok_evr[t] = op("s", lambda e, t=t: e.copy(out=rt_t[t % 2][:], in_=pT0[t % 2][:]),
                            waits=[tok_trr[t],
                                   tok_fstop.get(t - 2) if t >= 2 else None], inc="s")

        def b_f(t):
            for j in range(NCH):
                for h in range(2):
                    last = (j == NCH - 1 and h == 1)
                    tok = op("t", lambda e, j=j, h=h, t=t: e.matmul(
                        pC[:, h * 512:(h + 1) * 512],
                        rt_t[t % 2][:, j * P:(j + 1) * P],
                        tt_bf[:, j * D + h * 512: j * D + (h + 1) * 512],
                        start=(j == 0), stop=(j == NCH - 1)),
                        waits=[tok_evr[t] if (j == 0 and h == 0) else None,
                               (tok_g.get(t - 1) if (j == 0 and h == 0 and t >= 1) else None),
                               (gmx if (j == 0 and h == 0 and t == 0) else None)],
                        inc="t" if last else None)
                    if last:
                        tok_fstop[t] = tok

        def b_silu(t):
            tok_g[t] = op("s", lambda e, t=t: e.activation(
                out=g_t[t % 2][:], in_=pC[:], func=AF.Silu, scale=csa[:, 1:2]),
                waits=[tok_fstop[t], (tok_csa if t == 0 else None),
                       tok_add.get(t - 2) if t >= 2 else (setup_s if t < 2 else None)],
                inc="s")

        def b_ri(t):
            tok_ri[t] = op("v", lambda e, t=t: e.tensor_scalar(
                out=rint_t[:], in0=xn2_t[t % 2][:], scalar1=csa[:, 0:1], scalar2=None, op0=OP.mult),
                waits=[tok_n2[t], (tok_csa if t == 0 else None),
                       tok_rb.get(t - 1)],
                inc="v")
            tok_rb[t] = op("v", lambda e, t=t: e.tensor_copy(
                out=r_bf[:], in_=rint_t[:]),
                waits=[tok_ri[t], tok_trr.get(t - 1) if t >= 1 else setup_t,
                       (setup_v if t < 2 else None)],
                inc="v")

        def b_sg(t):
            tok_sg[t] = op("v", lambda e, t=t: e.tensor_tensor(
                out=g_t[t % 2][:], in0=s_sb[:], in1=g_t[t % 2][:], op=OP.mult),
                waits=[tok_g[t], tok_sev[t]], inc="v")
            tok_add[t] = op("g", lambda e, t=t: e.tensor_tensor(
                out=(gb_g, gb_b)[t % 2][:], in0=g_t[t % 2][:], in1=xg[t % 2][:], op=OP.add),
                waits=[tok_sg[t], tok_bx[t],
                       tok_ost.get(t - 2) if t >= 2 else tok_am.get((1, NT - 1)),
                       (tok_xn2.get((1, NT - 2 + t)) if t < 2 else None),
                       (tok_u.get((1, NT - 1)) if t < 2 else None)],
                inc="g")

        def b_ost(t):
            b, k = bk(t)
            tok_ost[t] = op("sync", lambda e, b=b, k=k, t=t: e.dma_start(
                out=out_d.ap()[b][k * P:(k + 1) * P, :], in_=(gb_g, gb_b)[t % 2][:]),
                waits=[tok_add[t]], inc="qo", inc_n=16)

        for i in range(-1, NTT + 2):
            if i == -1:
                b_ldn2(0)
                b_ldn2(1)
                b_ri(0)
                b_ldn2(2)
                b_ldx(0)
                b_trx(0)
                b_evx(0)
                continue
            if 0 <= i + 1 < NTT:
                b_trx(i + 1)
                b_evx(i + 1)
            if 0 <= i < NTT:
                b_s(i)
                b_trr(i)
            if 1 <= i + 1 < NTT:
                b_ri(i + 1)
            if 0 <= i + 3 < NTT:
                b_ldn2(i + 3)
            if 0 <= i < NTT:
                b_evr(i)
            if 0 <= i - 1 < NTT:
                b_f(i - 1)
                b_silu(i - 1)
                b_sg(i - 1)
                b_ost(i - 1)
            if 0 <= i < NTT:
                b_sev(i)
            if 0 <= i + 1 < NTT:
                b_ldx(i + 1)

        SCHED["sync"].append((tuple(("qs" + str(i), CTR["qs" + str(i)]) for i in range(DMA_FAM["qs"]))
                              + tuple(("qo" + str(i), CTR["qo" + str(i)]) for i in range(DMA_FAM["qo"])),
                              None, None, 0, 0))
        SCHED["g"].append(((("qg0", CTR["qg0"]),), None, None, 0, 0))
        SCHED["s"].append((tuple(("qa" + str(i), CTR["qa" + str(i)]) for i in range(DMA_FAM["qa"])),
                           None, None, 0, 0))

        def replay(eng_obj, name):
            waited = {}
            for waits, fn, incsem, incval, inc_n in SCHED[name]:
                for w in waits:
                    if callable(w):
                        w = w()
                    if w is None:
                        continue
                    sem, val = w
                    if val > waited.get(sem, 0):
                        eng_obj.wait_ge(SEM[sem], val)
                        waited[sem] = val
                if fn is None:
                    continue
                instr = fn(eng_obj)
                if incsem is not None:
                    instr.then_inc(SEM[incsem], inc_n)

        @block.sync
        def _(sync):
            replay(sync, "sync")

        @block.vector
        def _(vector):
            replay(vector, "v")

        @block.scalar
        def _(scalar):
            replay(scalar, "s")

        @block.tensor
        def _(tensor):
            replay(tensor, "t")

        @block.gpsimd
        def _(gpsimd):
            replay(gpsimd, "g")

    return nc


_cache = {}


def kernel(L0_attn_hidden_states, L0_attn_W_q, L0_ffn_weight, L0_ffn_gamma, L0_ffn_beta):
    x = np.ascontiguousarray(L0_attn_hidden_states, dtype=np.float32)
    wq = np.ascontiguousarray(L0_attn_W_q, dtype=np.float32)
    w = np.ascontiguousarray(L0_ffn_weight, dtype=np.float32)
    gamma = np.ascontiguousarray(L0_ffn_gamma, dtype=np.float32).reshape(1, D)
    beta = np.ascontiguousarray(L0_ffn_beta, dtype=np.float32).reshape(1, D)

    if "nc" not in _cache:
        _cache["nc"] = build()
    nc = _cache["nc"]

    in_maps = []
    for c in range(N_CORES):
        in_maps.append({
            "x_in": x[c * NB:(c + 1) * NB],
            "wq_in": wq,
            "w_in": w,
            "gamma_in": gamma,
            "beta_in": beta,
        })
    res = run_bass_kernel_spmd(nc, in_maps, core_ids=list(range(N_CORES)))
    out = np.concatenate([np.asarray(r["out_d"]) for r in res.results], axis=0)
    return out


# version 22
# speedup vs baseline: 7.0842x; 1.0176x over previous
import numpy as np
import ml_dtypes
import concourse.bass as bass
import concourse.mybir as mybir
import concourse.bass_utils as _bass_utils
from concourse.bass_utils import run_bass_kernel_spmd
from contextlib import ExitStack

if not getattr(_bass_utils, "_ldwopt_patched", False):
    _orig_run_command = _bass_utils.run_command

    def _run_command_ldwopt(cmd, *a, **kw):
        try:
            cmd = ["--enable-ldw-opt=true" if c == "--enable-ldw-opt=false" else c for c in cmd]
        except TypeError:
            pass
        return _orig_run_command(cmd, *a, **kw)

    _bass_utils.run_command = _run_command_ldwopt
    _bass_utils._ldwopt_patched = True

F32 = mybir.dt.float32
BF16 = mybir.dt.bfloat16
I32 = mybir.dt.int32
I8 = mybir.dt.int8
AF = mybir.ActivationFunctionType
OP = mybir.AluOpType
AX = mybir.AxisListType.X

P = 128
D = 1024
SEQ = 2048
NB = 2
NT = 16
NTT = NB * NT
NCH = 8
EPS = 1e-5
N_CORES = 8


def build():
    nc = bass.Bass("TRN2", target_bir_lowering=False)

    x_in = nc.dram_tensor("x_in", [NB, SEQ, D], F32, kind="ExternalInput")
    wq_in = nc.dram_tensor("wq_in", [D, D], F32, kind="ExternalInput")
    w_in = nc.dram_tensor("w_in", [D, D], F32, kind="ExternalInput")
    gamma_in = nc.dram_tensor("gamma_in", [1, D], F32, kind="ExternalInput")
    beta_in = nc.dram_tensor("beta_in", [1, D], F32, kind="ExternalInput")
    out_d = nc.dram_tensor("out_d", [NB, SEQ, D], F32, kind="ExternalOutput")
    xn2_d = nc.dram_tensor("xn2_d", [NB, SEQ, D], F32)
    cc_in0 = nc.dram_tensor("cc_in0", [1, 8], F32)
    cc_out0 = nc.dram_tensor("cc_out0", [1, 8], F32)
    cc_in1 = nc.dram_tensor("cc_in1", [1, 8], F32)
    cc_out1 = nc.dram_tensor("cc_out1", [1, 8], F32)
    delta_d = nc.dram_tensor("delta_d", [1, 1], F32)
    al_d = nc.dram_tensor("al_d", [1, 2], F32)
    csa_d = nc.dram_tensor("csa_d", [1, 2], F32)
    identb_d = nc.inline_tensor(np.eye(P, dtype=ml_dtypes.bfloat16), name="identb")
    identf_d = nc.inline_tensor(np.eye(P, dtype=np.float32), name="identf")
    eps_dd = nc.inline_tensor(
        np.concatenate([np.full((P, 1), EPS, np.float32),
                        np.full((P, 1), 1.0 / D, np.float32),
                        np.full((P, 1), EPS * EPS, np.float32)], axis=1), name="epsc")

    SCHED = {"sync": [], "v": [], "s": [], "t": [], "g": []}
    DMA_FAM = {"qs": 4, "qo": 2, "qa": 4, "qg": 2}
    SEM_NAMES = ["v", "s", "t", "g", "cc"] + [f + str(i) for f, n in DMA_FAM.items() for i in range(n)]
    CTR = {n: 0 for n in SEM_NAMES}
    FAM_ROT = {f: 0 for f in DMA_FAM}

    def op(eng, fn, waits=(), inc=None, inc_n=1):
        waits = [w for w in waits if w is not None]
        assert all(callable(w) or (isinstance(w, tuple) and len(w) == 2) for w in waits)
        if inc in DMA_FAM:
            fam = inc
            inc = fam + str(FAM_ROT[fam] % DMA_FAM[fam])
            FAM_ROT[fam] += 1
            waits.append((inc, CTR[inc]))
        tok = None
        if inc is not None:
            CTR[inc] += inc_n
            tok = (inc, CTR[inc])
        SCHED[eng].append((tuple(waits), fn, inc, CTR[inc] if inc else 0, inc_n))
        return tok

    def now(sem):
        return (sem, CTR[sem])

    es = ExitStack()
    with es:
        S_ = lambda nm, shp, dt=F32: es.enter_context(nc.sbuf_tensor(nm, shp, dt))
        xnT = S_("xnT", [P, NTT * D], BF16)
        gA = S_("gA", [P, NCH * D], BF16)
        gB = S_("gB", [P, NCH * D], BF16)
        gC = S_("gC", [P, NCH * D], BF16)
        wq_bf = S_("wq_bf", [P, NCH * D], BF16)
        tt_bf = S_("tt_bf", [P, NCH * D], BF16)
        gb_g = S_("gb_g", [P, D])
        gb_b = S_("gb_b", [P, D])
        xg = [S_(f"xg{i}", [P, D]) for i in range(3)]
        xn_t = [S_(f"xn_t{i}", [P, D]) for i in range(3)]
        xn2_t = [S_("xn2a", [P, D]), S_("xn2b", [P, D])]
        g_t = [S_("g0", [P, D]), S_("g1", [P, D])]
        rint_t = S_("rint", [P, D], I8)
        r_bf = S_("rbf", [P, D], BF16)
        xnf_t = [S_("xnf0", [P, D], BF16), S_("xnf1", [P, D], BF16)]
        rt_t = [S_("rt0", [P, D], BF16), S_("rt1", [P, D], BF16)]
        s_sb = S_("ssb", [P, D], BF16)
        bn6 = [S_("bn6", [P, 12])] * 2
        stt = S_("stt", [P, 48])
        amall = S_("amall", [P, 32])
        awsums = S_("awsums", [P, 8])
        nums = S_("nums", [P, 8])
        dens = S_("dens", [P, 8])
        small = S_("small", [P, 14])
        csa = S_("csa", [P, 2])
        delta_t = S_("delta_t", [P, 4])
        identb = S_("identb_s", [P, P], BF16)
        identf = S_("identf_s", [P, P])

        pA = es.enter_context(nc.psum_tensor("pA", [P, D], F32))
        pB = es.enter_context(nc.psum_tensor("pB", [P, D], F32))
        pC = es.enter_context(nc.psum_tensor("pC", [P, D], F32))
        pT0 = [es.enter_context(nc.psum_tensor("pT0a", [P, D], BF16)),
               es.enter_context(nc.psum_tensor("pT0b", [P, D], BF16))]
        pG = [pA, pB, pC]

        SEM = {}
        for s in SEM_NAMES:
            SEM[s] = es.enter_context(nc.semaphore("sem_" + s))
        block = es.enter_context(nc.Block())

        op("g", lambda e: e.dma_start(out=identb[:], in_=identb_d[:]), inc="qg", inc_n=16)
        op("g", lambda e: e.dma_start(out=identf[:], in_=identf_d[:]), inc="qg", inc_n=16)
        op("g", lambda e: e.dma_start(out=delta_t[:, 1:4], in_=eps_dd[:]), inc="qg", inc_n=16)
        op("g", lambda e: e.dma_start(out=gb_g[:], in_=gamma_in.ap().partition_broadcast(P)), inc="qg", inc_n=16)
        op("g", lambda e: e.dma_start(out=gb_b[:], in_=beta_in.ap().partition_broadcast(P)), inc="qg", inc_n=16)
        tok_consts = (now("qg0"), now("qg1"))

        tok_absw = [None] * NCH
        tok_wld = [None] * NCH
        for c in range(NCH):
            par = c % 2
            tok_wld[c] = op("sync", lambda e, c=c, par=par: e.dma_start(
                out=g_t[par][:], in_=w_in.ap()[c * P:(c + 1) * P, :]),
                waits=[tok_absw[c - 2] if c >= 2 else None], inc="qs", inc_n=16)
            tok_absw[c] = op("s", lambda e, c=c, par=par: e.activation(
                out=xn_t[0][:], in_=g_t[par][:], func=AF.Abs, accum_out=awsums[:, c:c + 1]),
                waits=[tok_wld[c], (tok_absw[c - 1] if c >= 1 else None)], inc="s")
        tok_rsum = op("v", lambda e: e.reduce_sum(out=small[:, 0:1], in_=awsums[:, 0:8], axis=AX),
                      waits=[tok_absw[NCH - 1]], inc="v")
        tok_trs = op("t", lambda e: e.transpose(pA[0:1, 0:P], small[:, 0:1], identf[:]),
                     waits=[tok_rsum, tok_consts[0], tok_consts[1]], inc="t")
        tok_tot = op("v", lambda e: e.reduce_sum(out=small[0:1, 1:2], in_=pA[0:1, 0:P], axis=AX),
                     waits=[tok_trs], inc="v")
        tok_del = op("v", lambda e: e.tensor_scalar(out=small[0:1, 2:3], in0=small[0:1, 1:2],
                                                    scalar1=float(0.7 / (D * D)), scalar2=None, op0=OP.mult),
                     waits=[tok_tot], inc="v")
        tok_dst = op("g", lambda e: e.dma_start(out=delta_d[0:1, 0:1], in_=small[0:1, 2:3]),
                     waits=[tok_del], inc="qg", inc_n=16)
        tok_dbk = op("g", lambda e: e.dma_start(out=delta_t[:, 0:1], in_=delta_d.ap()[0:1, 0:1].partition_broadcast(P)),
                     waits=[tok_dst], inc="qg", inc_n=16)

        tok_T = [None] * NCH
        tok_trT = [None] * NCH
        tok_evT = [None] * NCH
        tok_num = [None] * NCH
        tok_sgn = [None] * NCH
        tok_mkl = [None] * NCH
        tok_wld2 = [None] * NCH
        absw_b = [xn_t[0], xn_t[2]]
        sgn_b = [r_bf, rt_t[1]]

        def p2_s0(c):
            par = c % 2
            tok_wld2[c] = op("sync", lambda e, c=c, par=par: e.dma_start(
                out=g_t[par][:], in_=w_in.ap()[c * P:(c + 1) * P, :]),
                waits=[tok_sgn[c - 2] if c >= 2 else tok_absw[NCH - 1],
                       tok_mkl[c - 2] if c >= 2 else None],
                inc="qs", inc_n=16)

        def p2_s1(c):
            par = c % 2
            op("s", lambda e, c=c, par=par: e.activation(
                out=absw_b[par][:], in_=g_t[par][:], func=AF.Abs),
                waits=[tok_wld2[c], tok_num[c - 2] if c >= 2 else tok_rsum], inc="s")
            tok_sgn[c] = op("s", lambda e, c=c, par=par: e.activation(
                out=sgn_b[par][:], in_=g_t[par][:], func=AF.Sign),
                waits=[tok_T[c - 2] if c >= 2 else None], inc="s")

        def p2_s2(c):
            par = c % 2
            tok_mkl[c] = op("v", lambda e, c=c, par=par: e.tensor_scalar(
                out=xnf_t[0][:], in0=absw_b[par][:], scalar1=delta_t[:, 0:1], scalar2=0.0,
                op0=OP.is_gt, op1=OP.add, accum_out=dens[:, c:c + 1]),
                waits=[tok_sgn[c], tok_dbk, tok_T[c - 1] if c >= 1 else None], inc="v")
            tok_num[c] = op("v", lambda e, c=c, par=par: e.scalar_tensor_tensor(
                out=xn_t[1][:], in0=absw_b[par][:], scalar=1.0, in1=xnf_t[0][:],
                op0=OP.mult, op1=OP.mult, accum_out=nums[:, c:c + 1]),
                waits=[tok_mkl[c]], inc="v")
            tok_T[c] = op("v", lambda e, c=c, par=par: e.tensor_tensor(
                out=rt_t[0][:], in0=sgn_b[par][:], in1=xnf_t[0][:], op=OP.mult),
                waits=[tok_num[c], tok_trT[c - 1] if c >= 1 else None], inc="v")

        def p2_s3(c):
            for j in range(NCH):
                tok_trT[c] = op("t", lambda e, j=j: e.transpose(
                    pT0[0][:, j * P:(j + 1) * P], rt_t[0][:, j * P:(j + 1) * P], identb[:]),
                    waits=[tok_T[c] if j == 0 else None,
                           tok_evT[c - 1] if (j == 0 and c >= 1) else None,
                           (tok_trs if (j == 0 and c == 0) else None)],
                    inc="t" if j == NCH - 1 else None)
            tok_evT[c] = op("s", lambda e, c=c: e.copy(
                out=tt_bf[:].rearrange("p (j o) -> p j o", o=D)[:, :, c * P:(c + 1) * P],
                in_=pT0[0][:].rearrange("p (j o) -> p j o", o=P)),
                waits=[tok_trT[c]], inc="s")

        for i in range(NCH + 3):
            if i < NCH:
                p2_s0(i)
            if 0 <= i - 1 < NCH:
                p2_s1(i - 1)
            if 0 <= i - 3 < NCH:
                p2_s3(i - 3)
            if 0 <= i - 2 < NCH:
                p2_s2(i - 2)

        tok_ns = op("v", lambda e: e.reduce_sum(out=small[:, 3:4], in_=nums[:, 0:8], axis=AX),
                    waits=[tok_num[NCH - 1]], inc="v")
        tok_ds = op("v", lambda e: e.reduce_sum(out=small[:, 4:5], in_=dens[:, 0:8], axis=AX),
                    waits=[tok_ns], inc="v")
        tok_tr2 = op("t", lambda e: e.transpose(pA[0:2, 0:P], small[:, 3:5], identf[:]),
                     waits=[tok_ds], inc="t")
        tok_nd = op("v", lambda e: e.reduce_sum(out=small[0:2, 5:6], in_=pA[0:2, 0:P], axis=AX),
                    waits=[tok_tr2], inc="v")
        tok_alst = op("g", lambda e: e.dma_start(out=al_d[0:1, 0:2], in_=small[0:2, 5:6]),
                      waits=[tok_nd], inc="qg", inc_n=16)
        tok_albk = op("g", lambda e: e.dma_start(out=small[0:1, 6:8], in_=al_d[0:1, 0:2]),
                      waits=[tok_alst], inc="qg", inc_n=16)
        tok_dmx = op("v", lambda e: e.tensor_scalar(out=small[0:1, 8:9], in0=small[0:1, 7:8],
                                                    scalar1=1.0, scalar2=None, op0=OP.max),
                     waits=[tok_albk], inc="v")
        tok_rd = op("v", lambda e: e.reciprocal(out=small[0:1, 9:10], in_=small[0:1, 8:9]),
                    waits=[tok_dmx], inc="v")
        tok_alpha = op("v", lambda e: e.tensor_tensor(out=small[0:1, 10:11], in0=small[0:1, 6:7],
                                                      in1=small[0:1, 9:10], op=OP.mult),
                       waits=[tok_rd], inc="v")

        tok_wqc = [None] * NCH
        tok_wqc[NCH - 1] = op("g", lambda e: e.dma_start(
            out=wq_bf[:].rearrange("p (c i) -> p c i", i=D),
            in_=wq_in.ap().rearrange("(c p) i -> p c i", p=P)),
            inc="qg", inc_n=16)
        tok_wql = tok_wqc[NCH - 1]

        setup_v, setup_s, setup_t = now("v"), now("s"), now("t")

        tok_x = {}
        tok_aggrD = {}
        tok_ti = {}
        tok_lv2 = {}
        tok_diff = {}
        tok_txn = {}
        tok_cast = {}
        tok_r2D = {}
        tok_u = {}
        tok_mul = {}
        tok_xn2 = {}
        tok_am = {}
        tok_gst = {}
        tok_gstop = {}
        tok_gev = {}
        tok_mstop = {}
        tok_mev = {}
        tok_ccgo = {}
        tok_ccred = {}
        tok_gm_hold = {}

        def stC(k, lo, hi):
            return stt[:, (k % 4) * 12 + lo:(k % 4) * 12 + hi]

        def a_load(b, k):
            par = (b * NT + k) % 3
            tok_x[(b, k)] = op("sync", lambda e, b=b, k=k, par=par: e.dma_start(
                out=xg[par][:], in_=x_in.ap()[b][k * P:(k + 1) * P, :]),
                waits=[tok_txn.get((b, k - 3)) if k >= 3 else (tok_txn.get((b - 1, NT - 3 + k)) if b else None),
                       tok_cast.get((b, k - 3)) if k >= 3 else (tok_cast.get((b - 1, NT - 3 + k)) if b else None),
                       tok_x.get((b, k - 3)) if k >= 3 else (tok_x.get((b - 1, NT - 3 + k)) if b else None)],
                inc="qs", inc_n=16)

        def a_s1(b, k):
            par = k % 2
            t1 = op("v", lambda e, par=par, b=b, k=k: e.bn_stats(out=bn6[par][:, 0:6], in_=xg[(b * NT + k) % 3][:, 0:512]),
                    waits=[tok_x[(b, k)],
                           tok_aggrD.get((b, k - 1)) or (tok_aggrD.get((b - 1, NT - 1)) if b else None)], inc="v")
            t2 = op("v", lambda e, par=par, b=b, k=k: e.bn_stats(out=bn6[par][:, 6:12], in_=xg[(b * NT + k) % 3][:, 512:1024]),
                    waits=[], inc="v")
            tok_aggrD[(b, k)] = op("v", lambda e, par=par, k=k: e.bn_aggr(out=stC(k, 0, 2), in_=bn6[par][:, 0:12]),
                                   waits=[t2, tok_r2D.get((b, k - 4)) or (tok_r2D.get((b - 1, NT - 4 + k)) if b else None)],
                                   inc="v")

        def a_s2(b, k):
            tl = op("s", lambda e, k=k: e.activation(out=stC(k, 2, 3), in_=stC(k, 1, 2),
                                                     func=AF.Ln, bias=delta_t[:, 1:2]),
                    waits=[tok_aggrD[(b, k)],
                           tok_diff.get((b, k - 4)) or (tok_diff.get((b - 1, NT - 4 + k)) if b else None)], inc="s")
            tok_ti[(b, k)] = op("s", lambda e, k=k: e.activation(out=stC(k, 3, 4), in_=stC(k, 2, 3),
                                                                 func=AF.Exp, scale=-0.5),
                                waits=[tl], inc="s")
            tok_lv2[(b, k)] = op("s", lambda e, k=k: e.activation(out=stC(k, 4, 5), in_=stC(k, 1, 2),
                                                                  func=AF.Ln, scale=float(1.0 + EPS),
                                                                  bias=delta_t[:, 3:4]),
                                waits=[tok_ti[(b, k)]], inc="s")

        def a_s3(b, k):
            tok_txn[(b, k)] = op("v", lambda e, b=b, k=k: e.tensor_scalar(
                out=xn_t[(b * NT + k) % 3][:], in0=xg[(b * NT + k) % 3][:], scalar1=stC(k, 0, 1), scalar2=stC(k, 3, 4),
                op0=OP.subtract, op1=OP.mult),
                waits=[tok_ti[(b, k)],
                       tok_u.get((b, k - 3)) or (tok_u.get((b - 1, NT - 3 + k)) if b else None),
                       setup_s if (b == 0 and k == 0) else None], inc="v")
            tok_diff[(b, k)] = op("v", lambda e, k=k: e.tensor_tensor(
                out=stC(k, 5, 6), in0=stC(k, 2, 3), in1=stC(k, 4, 5), op=OP.subtract),
                waits=[tok_lv2[(b, k)],
                       tok_r2D.get((b, k - 4)) or (tok_r2D.get((b - 1, NT - 4 + k)) if b else None)], inc="v")

        def a_s4(b, k):
            boff = b * NT * D
            tok_cast[(b, k)] = op("s", lambda e, b=b, k=k, boff=boff: e.activation(
                out=xnT[:, boff + k * D: boff + (k + 1) * D], in_=xn_t[(b * NT + k) % 3][:],
                func=AF.Copy),
                waits=[tok_txn[(b, k)]], inc="s")
            tok_r2D[(b, k)] = op("s", lambda e, k=k: e.activation(
                out=stC(k, 6, 7), in_=stC(k, 5, 6), func=AF.Exp, scale=0.5),
                waits=[tok_cast[(b, k)], tok_diff[(b, k)],
                       tok_u.get((b, k - 4)) or (tok_u.get((b - 1, NT - 4 + k)) if b else None)],
                inc="s")

        def a_s7(b, k):
            par = k % 2
            tok_u[(b, k)] = op("v", lambda e, b=b, k=k, par=par: e.scalar_tensor_tensor(
                out=xn2_t[par][:], in0=xn_t[(b * NT + k) % 3][:], scalar=stC(k, 6, 7),
                in1=gb_g[:], op0=OP.mult, op1=OP.mult),
                waits=[tok_r2D[(b, k)],
                       tok_gst.get((b, k - 2)) if k >= 2 else (tok_gst.get((b - 1, NT - 2 + k)) if b else None),
                       tok_consts[0] if (b == 0 and k == 0) else None,
                       tok_consts[1] if (b == 0 and k == 0) else None], inc="v")
            tok_xn2[(b, k)] = op("g", lambda e, par=par: e.tensor_tensor(
                out=xn2_t[par][:], in0=xn2_t[par][:], in1=gb_b[:], op=OP.add),
                waits=[tok_u[(b, k)]], inc="g")
        def a_s8(b, k):
            par = k % 2
            tok_am[(b, k)] = op("v", lambda e, par=par, b=b, k=k: e.tensor_reduce(
                out=amall[:, b * NT + k: b * NT + k + 1], in_=xn2_t[par][:], axis=AX,
                op=OP.max, apply_absolute_value=True),
                waits=[tok_xn2[(b, k)]], inc="v")
            tok_gst[(b, k)] = op("s", lambda e, b=b, k=k, par=par: e.dma_start(
                out=xn2_d.ap()[b][k * P:(k + 1) * P, :], in_=xn2_t[par][:]),
                waits=[tok_am[(b, k)]], inc="qa", inc_n=16)

        def a_pe(b, k):
            boff = b * NT * D
            for c in range(3):
                for h in range(2):
                    last = (k == NT - 1 and h == 1)
                    tok = op("t", lambda e, b=b, k=k, c=c, h=h, boff=boff: e.matmul(
                        pG[c][:, h * 512:(h + 1) * 512],
                        xnT[:, boff + k * D + c * P: boff + k * D + (c + 1) * P],
                        xnT[:, boff + k * D + h * 512: boff + k * D + (h + 1) * 512],
                        start=(k == 0), stop=(k == NT - 1)),
                        waits=[tok_cast[(b, k)] if (c == 0 and h == 0) else None,
                               (tok_gev.get((b - 1, 7)) if (b and k == 0 and c == 0 and h == 0) else None),
                               (tok_mev.get((b - 1, NCH - 1)) if (b and k == 0 and c == 0 and h == 0) else None),
                               (tok_ccred.get(b - 1) if (b and k == 0 and c == 0 and h == 0) else None),
                               (setup_t if (b == 0 and k == 0 and c == 0 and h == 0) else None)],
                        inc="t" if last else None)
                    if last:
                        tok_gstop[(b, c)] = tok

        def phaseA(b, defer=None):
            defer = defer or []
            for i in range(NT + 5):
                if i < NT:
                    a_load(b, i)
                if 0 <= i - 1 < NT:
                    a_s1(b, i - 1)
                    a_s2(b, i - 1)
                if 0 <= i - 2 < NT:
                    a_s3(b, i - 2)
                    a_s4(b, i - 2)
                    if b == 0:
                        a_pe(b, i - 2)
                if 0 <= i - 4 < NT:
                    a_s7(b, i - 4)
                if 0 <= i - 5 < NT:
                    a_s8(b, i - 5)
                if i >= 1 and defer:
                    defer.pop(0)()
            while defer:
                defer.pop(0)()

        def g_rest_m_cc(b, defer=None):
            boff = b * NT * D
            gdst = gA
            mdst = gB if b == 0 else gC
            ccin = cc_in0 if b == 0 else cc_in1
            ccout = cc_out0 if b == 0 else cc_out1
            DQ = defer if defer is not None else []
            def emit(fn):
                if defer is None:
                    fn()
                else:
                    DQ.append(fn)
            if b == 0:
                for c in range(3):
                    emit(lambda c=c: tok_gev.__setitem__((b, c), op("s", lambda e, c=c, gdst=gdst: e.copy(
                        out=gdst[:, c * D:(c + 1) * D], in_=pG[c][:]),
                        waits=[tok_gstop[(b, c)]], inc="s")))
            hold = {}
            def _amr():
                hold['amr'] = op("v", lambda e, b=b: e.reduce_max(out=small[:, 11:12],
                                                                  in_=amall[:, b * NT:(b + 1) * NT], axis=AX),
                                 waits=[tok_am[(b, NT - 1)],
                                        (tok_gm_hold[0]['amt'] if b == 1 else None)], inc="v")
            emit(_amr)
            if b == 0:
                hold['amt'] = op("t", lambda e, b=b: e.transpose(pC[0:1, 0:P], small[:, 11:12], identf[:]),
                                 waits=[lambda: hold['amr'], lambda: tok_gev[(0, 2)]], inc="t")

            def _gm():
                hold['gm'] = op("v", lambda e, b=b: e.reduce_max(out=small[0:1, 12:13],
                                                                 in_=pC[0:1, 0:P], axis=AX),
                                waits=[lambda: hold['amt'],
                                       (tok_ccred[0] if b == 1 else None)], inc="v")
                tok_ccred[b] = op("v", lambda e, b=b: e.tensor_copy(
                    out=nums[0:1, 0:8],
                    in_=small[0:1, 12:13].broadcast_to((1, 8))),
                    waits=[hold['gm'], tok_ccgo.get(b - 1), (tok_ns if b == 0 else None)], inc="v")
                cci = op("g", lambda e, ccin=ccin: e.dma_start(out=ccin[0:1, 0:8], in_=nums[0:1, 0:8]),
                         waits=[tok_ccred[b]], inc="qg", inc_n=16)
                tok_ccgo[b] = op("g", lambda e, ccin=ccin, ccout=ccout: e.collective_compute(
                    "AllReduce", OP.max, replica_groups=[list(range(N_CORES))],
                    ins=[ccin.ap().opt()], outs=[ccout.ap().opt()]),
                    waits=[cci], inc="cc", inc_n=1)
            emit(_gm)
            tok_gm_hold[b] = hold
            rem = list(range(3, NCH)) if b == 0 else list(range(NCH))
            BANKS = [0, 1, 0, 1, 2, 0, 1, 2, 0, 1, 2, 0, 1, 2, 0, 1]
            first_pc = BANKS.index(2)
            for bi, c in enumerate(rem):
                if bi == first_pc and b == 1:
                    hold['amt'] = op("t", lambda e, b=b: e.transpose(pC[0:1, 0:P], small[:, 11:12], identf[:]),
                                     waits=[lambda: hold['amr'], lambda: tok_mev[(0, NCH - 1)]],
                                     inc="t")
                pm = pG[BANKS[bi]]
                previ = None
                for q in range(bi - 1, -1, -1):
                    if BANKS[q] == BANKS[bi]:
                        previ = q
                        break
                if previ is not None:
                    prev_user = (lambda b=b, c=rem[previ]: tok_gev[(b, c)])
                elif b == 0:
                    prev_user = (lambda bk=BANKS[bi]: tok_gev[(0, bk)])
                else:
                    prev_user = (lambda: tok_mev[(0, NCH - 1)])
                for k in range(NT):
                    for h in range(2):
                        last = (k == NT - 1 and h == 1)
                        tok = op("t", lambda e, k=k, c=c, h=h, pm=pm, boff=boff: e.matmul(
                            pm[:, h * 512:(h + 1) * 512],
                            xnT[:, boff + k * D + c * P: boff + k * D + (c + 1) * P],
                            xnT[:, boff + k * D + h * 512: boff + k * D + (h + 1) * 512],
                            start=(k == 0), stop=(k == NT - 1)),
                            waits=[prev_user if (k == 0 and h == 0) else None,
                                   (hold['amt'] if (k == 0 and h == 0 and bi == first_pc) else None),
                                   ((lambda b=b: tok_gm_hold[b]['gm']) if (k == 0 and h == 0 and bi == first_pc) else None),
                                   (tok_cast[(b, NT - 1)] if (k == 0 and h == 0 and bi == 0 and b == 1) else None)],
                            inc="t" if last else None)
                        if last:
                            tok_gstop[(b, c)] = tok
                emit(lambda c=c, pm=pm: tok_gev.__setitem__((b, c), op(
                    "s", lambda e, c=c, pm=pm, gdst=gdst: e.copy(
                        out=gdst[:, c * D:(c + 1) * D], in_=pm[:]),
                    waits=[tok_gstop[(b, c)]], inc="s")))
            for hc in range(NCH):
                pm = pG[BANKS[(len(rem) + hc) % len(BANKS)]]
                for oc in range(NCH):
                    for h in range(2):
                        last = (oc == NCH - 1 and h == 1)
                        tok = op("t", lambda e, hc=hc, oc=oc, h=h, pm=pm, gdst=gdst: e.matmul(
                            pm[:, h * 512:(h + 1) * 512],
                            wq_bf[:, oc * D + hc * P: oc * D + (hc + 1) * P],
                            gdst[:, oc * D + h * 512: oc * D + (h + 1) * 512],
                            start=(oc == 0), stop=(oc == NCH - 1)),
                            waits=[((lambda b=b: tok_gev[(b, NCH - 1)]) if (hc == 0 and oc == 0 and h == 0) else None),
                                   (tok_wqc[NCH - 1] if (hc == 0 and oc == 0 and h == 0) else None),
                                   ((lambda b=b, hc=hc: tok_gev[(b, NCH - 3 + hc)]) if (hc in (1, 2) and oc == 0 and h == 0) else None),
                                   ((lambda b=b, hc=hc: tok_mev[(b, hc - 3)]) if (hc >= 3 and oc == 0 and h == 0) else None)],
                            inc="t" if last else None)
                        if last:
                            tok_mstop[(b, hc)] = tok
                emit(lambda hc=hc, pm=pm: tok_mev.__setitem__((b, hc), op(
                    "s", lambda e, hc=hc, pm=pm, mdst=mdst: e.copy(
                        out=mdst[:, hc * D:(hc + 1) * D], in_=pm[:]),
                    waits=[tok_mstop[(b, hc)]], inc="s")))

        phaseA(0)
        DQ0 = []
        g_rest_m_cc(0, defer=DQ0)
        phaseA(1, defer=DQ0)
        g_rest_m_cc(1)

        gl0 = op("g", lambda e: e.dma_start(out=small[0:1, 13:14], in_=cc_out0[0:1, 0:1]),
                 waits=[tok_ccgo[0]], inc="qg", inc_n=16)
        gl1 = op("g", lambda e: e.dma_start(out=awsums[0:1, 0:1], in_=cc_out1[0:1, 0:1]),
                 waits=[tok_ccgo[1], tok_ccred[1]], inc="qg", inc_n=16)
        gmx = op("v", lambda e: e.tensor_tensor(out=awsums[0:1, 1:2], in0=small[0:1, 13:14],
                                                in1=awsums[0:1, 0:1], op=OP.max),
                 waits=[gl0, gl1, tok_gm_hold[1]['gm']], inc="v")
        rec = op("v", lambda e: e.reciprocal(out=awsums[0:1, 2:3], in_=awsums[0:1, 1:2]),
                 waits=[gmx], inc="v")
        tc = op("v", lambda e: e.tensor_scalar(out=awsums[0:1, 3:4], in0=awsums[0:1, 2:3],
                                               scalar1=127.0, scalar2=None, op0=OP.mult),
                waits=[rec], inc="v")
        ts1 = op("v", lambda e: e.tensor_scalar(out=awsums[0:1, 5:6], in0=awsums[0:1, 1:2],
                                                scalar1=float(1.0 / 127.0), scalar2=None, op0=OP.mult),
                 waits=[tc], inc="v")
        tsa = op("v", lambda e: e.tensor_tensor(out=awsums[0:1, 4:5], in0=awsums[0:1, 5:6],
                                                in1=small[0:1, 10:11], op=OP.mult),
                 waits=[ts1, tok_alpha], inc="v")
        cst = op("g", lambda e: e.dma_start(out=csa_d[0:1, 0:2], in_=awsums[0:1, 3:5]),
                 waits=[tsa], inc="qg", inc_n=16)
        tok_csa = op("g", lambda e: e.dma_start(out=csa[:, 0:2],
                                                in_=csa_d.ap()[0:1, 0:2].partition_broadcast(P)),
                     waits=[cst], inc="qg", inc_n=16)

        A_end_v = now("v")

        tok_bx = {}
        tok_n2 = {}
        tok_trx = {}
        tok_evx = {}
        tok_sstop = {}
        tok_sev = {}
        tok_ri = {}
        tok_rb = {}
        tok_trr = {}
        tok_evr = {}
        tok_fstop = {}
        tok_g = {}
        tok_sg = {}
        tok_add = {}
        tok_ost = {}

        def bk(t):
            return t // NT, t % NT

        def b_ldx(t):
            b, k = bk(t)
            tok_bx[t] = op("sync", lambda e, b=b, k=k: e.dma_start(
                out=xg[t % 2][:], in_=x_in.ap()[b][k * P:(k + 1) * P, :]),
                waits=[tok_add.get(t - 2) if t >= 2 else tok_txn.get((1, NT - 1)),
                       (tok_cast.get((1, NT - 1)) if t < 2 else None),
                       tok_bx.get(t - 2) if t >= 2 else tok_x.get((1, NT - 1))],
                inc="qs", inc_n=16)

        def b_ldn2(t):
            b, k = bk(t)
            tok_n2[t] = op("s", lambda e, b=b, k=k, t=t: e.dma_start(
                out=xn2_t[t % 2][:], in_=xn2_d.ap()[b][k * P:(k + 1) * P, :]),
                waits=[tok_ri.get(t - 2) if t >= 2 else tok_gst.get((1, NT - 2 + t)),
                       (tok_am.get((1, NT - 2 + t)) if t < 2 else None),
                       tok_n2.get(t - 2) if t >= 2 else tok_gst.get((1, NT - 1))],
                inc="qa", inc_n=16)

        def b_trx(t):
            b, k = bk(t)
            boff = b * NT * D
            for j in range(NCH):
                tok_trx[t] = op("t", lambda e, j=j, boff=boff, k=k, t=t: e.transpose(
                    pT0[t % 2][:, j * P:(j + 1) * P],
                    xnT[:, boff + k * D + j * P: boff + k * D + (j + 1) * P], identb[:]),
                    waits=[(tok_evr.get(t - 2) if t >= 2 else tok_evT[NCH - 1]) if j == 0 else None,
                           (A_end_v if (t == 0 and j == 0) else None),
                           (tok_mev.get((1, NCH - 1)) if (t == 0 and j == 0) else None)],
                    inc="t" if j == NCH - 1 else None)

        def b_evx(t):
            tok_evx[t] = op("s", lambda e, t=t: e.copy(out=xnf_t[t % 2][:], in_=pT0[t % 2][:]),
                            waits=[tok_trx[t],
                                   tok_sstop.get(t - 2) if t >= 2 else None,
                                   (setup_v if t == 0 else None)], inc="s")

        def b_s(t):
            b, k = bk(t)
            msrc = gB if b == 0 else gC
            pm = pG[t % 2]
            for j in range(NCH):
                for h in range(2):
                    last = (j == NCH - 1 and h == 1)
                    tok = op("t", lambda e, j=j, h=h, t=t, msrc=msrc, pm=pm: e.matmul(
                        pm[:, h * 512:(h + 1) * 512],
                        xnf_t[t % 2][:, j * P:(j + 1) * P],
                        msrc[:, j * D + h * 512: j * D + (h + 1) * 512],
                        start=(j == 0), stop=(j == NCH - 1)),
                        waits=[tok_evx[t] if (j == 0 and h == 0) else None,
                               (tok_sev.get(t - 2) if (j == 0 and h == 0 and t >= 2) else None),
                               ((lambda b=b: tok_mev[(b, NCH - 1)]) if (j == 0 and h == 0 and k == 0) else None)],
                        inc="t" if last else None)
                    if last:
                        tok_sstop[t] = tok

        def b_sev(t):
            tok_sev[t] = op("s", lambda e, t=t: e.copy(
                out=s_sb[:], in_=pG[t % 2][:]),
                waits=[tok_sstop[t], tok_sg.get(t - 1)], inc="s")

        def b_trr(t):
            for j in range(NCH):
                tok_trr[t] = op("t", lambda e, j=j, t=t: e.transpose(
                    pT0[t % 2][:, j * P:(j + 1) * P], r_bf[:, j * P:(j + 1) * P], identb[:]),
                    waits=[(tok_evx[t] if j == 0 else None),
                           (tok_rb[t] if j == 0 else None)],
                    inc="t" if j == NCH - 1 else None)

        def b_evr(t):
            tok_evr[t] = op("s", lambda e, t=t: e.copy(out=rt_t[t % 2][:], in_=pT0[t % 2][:]),
                            waits=[tok_trr[t],
                                   tok_fstop.get(t - 2) if t >= 2 else None], inc="s")

        def b_f(t):
            for j in range(NCH):
                for h in range(2):
                    last = (j == NCH - 1 and h == 1)
                    tok = op("t", lambda e, j=j, h=h, t=t: e.matmul(
                        pC[:, h * 512:(h + 1) * 512],
                        rt_t[t % 2][:, j * P:(j + 1) * P],
                        tt_bf[:, j * D + h * 512: j * D + (h + 1) * 512],
                        start=(j == 0), stop=(j == NCH - 1)),
                        waits=[tok_evr[t] if (j == 0 and h == 0) else None,
                               (tok_g.get(t - 1) if (j == 0 and h == 0 and t >= 1) else None),
                               (gmx if (j == 0 and h == 0 and t == 0) else None)],
                        inc="t" if last else None)
                    if last:
                        tok_fstop[t] = tok

        def b_silu(t):
            tok_g[t] = op("s", lambda e, t=t: e.activation(
                out=g_t[t % 2][:], in_=pC[:], func=AF.Silu, scale=csa[:, 1:2]),
                waits=[tok_fstop[t], (tok_csa if t == 0 else None),
                       tok_add.get(t - 2) if t >= 2 else (setup_s if t < 2 else None)],
                inc="s")

        def b_ri(t):
            tok_ri[t] = op("v", lambda e, t=t: e.tensor_scalar(
                out=rint_t[:], in0=xn2_t[t % 2][:], scalar1=csa[:, 0:1], scalar2=None, op0=OP.mult),
                waits=[tok_n2[t], (tok_csa if t == 0 else None),
                       tok_rb.get(t - 1)],
                inc="v")
            tok_rb[t] = op("v", lambda e, t=t: e.tensor_copy(
                out=r_bf[:], in_=rint_t[:]),
                waits=[tok_ri[t], tok_trr.get(t - 1) if t >= 1 else setup_t,
                       (setup_v if t < 2 else None)],
                inc="v")

        def b_sg(t):
            tok_sg[t] = op("v", lambda e, t=t: e.tensor_tensor(
                out=g_t[t % 2][:], in0=s_sb[:], in1=g_t[t % 2][:], op=OP.mult),
                waits=[tok_g[t], tok_sev[t]], inc="v")
            tok_add[t] = op("g", lambda e, t=t: e.tensor_tensor(
                out=(gb_g, gb_b)[t % 2][:], in0=g_t[t % 2][:], in1=xg[t % 2][:], op=OP.add),
                waits=[tok_sg[t], tok_bx[t],
                       tok_ost.get(t - 2) if t >= 2 else tok_am.get((1, NT - 1)),
                       (tok_xn2.get((1, NT - 2 + t)) if t < 2 else None),
                       (tok_u.get((1, NT - 1)) if t < 2 else None)],
                inc="g")

        def b_ost(t):
            b, k = bk(t)
            tok_ost[t] = op("sync", lambda e, b=b, k=k, t=t: e.dma_start(
                out=out_d.ap()[b][k * P:(k + 1) * P, :], in_=(gb_g, gb_b)[t % 2][:]),
                waits=[tok_add[t]], inc="qo", inc_n=16)

        for i in range(-1, NTT + 2):
            if i == -1:
                b_ldn2(0)
                b_ldn2(1)
                b_ri(0)
                b_ldn2(2)
                b_ldx(0)
                b_trx(0)
                b_evx(0)
                continue
            if 0 <= i + 1 < NTT:
                b_trx(i + 1)
                b_evx(i + 1)
            if 0 <= i < NTT:
                b_s(i)
                b_trr(i)
            if 1 <= i + 1 < NTT:
                b_ri(i + 1)
            if 0 <= i + 3 < NTT:
                b_ldn2(i + 3)
            if 0 <= i < NTT:
                b_evr(i)
            if 0 <= i - 1 < NTT:
                b_f(i - 1)
                b_silu(i - 1)
                b_sg(i - 1)
                b_ost(i - 1)
            if 0 <= i < NTT:
                b_sev(i)
            if 0 <= i + 1 < NTT:
                b_ldx(i + 1)

        SCHED["sync"].append((tuple(("qs" + str(i), CTR["qs" + str(i)]) for i in range(DMA_FAM["qs"]))
                              + tuple(("qo" + str(i), CTR["qo" + str(i)]) for i in range(DMA_FAM["qo"])),
                              None, None, 0, 0))
        SCHED["g"].append(((("qg0", CTR["qg0"]),), None, None, 0, 0))
        SCHED["s"].append((tuple(("qa" + str(i), CTR["qa" + str(i)]) for i in range(DMA_FAM["qa"])),
                           None, None, 0, 0))

        def replay(eng_obj, name):
            waited = {}
            for waits, fn, incsem, incval, inc_n in SCHED[name]:
                for w in waits:
                    if callable(w):
                        w = w()
                    if w is None:
                        continue
                    sem, val = w
                    if val > waited.get(sem, 0):
                        eng_obj.wait_ge(SEM[sem], val)
                        waited[sem] = val
                if fn is None:
                    continue
                instr = fn(eng_obj)
                if incsem is not None:
                    instr.then_inc(SEM[incsem], inc_n)

        @block.sync
        def _(sync):
            replay(sync, "sync")

        @block.vector
        def _(vector):
            replay(vector, "v")

        @block.scalar
        def _(scalar):
            replay(scalar, "s")

        @block.tensor
        def _(tensor):
            replay(tensor, "t")

        @block.gpsimd
        def _(gpsimd):
            replay(gpsimd, "g")

    return nc


_cache = {}


def kernel(L0_attn_hidden_states, L0_attn_W_q, L0_ffn_weight, L0_ffn_gamma, L0_ffn_beta):
    x = np.ascontiguousarray(L0_attn_hidden_states, dtype=np.float32)
    wq = np.ascontiguousarray(L0_attn_W_q, dtype=np.float32)
    w = np.ascontiguousarray(L0_ffn_weight, dtype=np.float32)
    gamma = np.ascontiguousarray(L0_ffn_gamma, dtype=np.float32).reshape(1, D)
    beta = np.ascontiguousarray(L0_ffn_beta, dtype=np.float32).reshape(1, D)

    if "nc" not in _cache:
        _cache["nc"] = build()
    nc = _cache["nc"]

    in_maps = []
    for c in range(N_CORES):
        in_maps.append({
            "x_in": x[c * NB:(c + 1) * NB],
            "wq_in": wq,
            "w_in": w,
            "gamma_in": gamma,
            "beta_in": beta,
        })
    res = run_bass_kernel_spmd(nc, in_maps, core_ids=list(range(N_CORES)))
    out = np.concatenate([np.asarray(r["out_d"]) for r in res.results], axis=0)
    return out
